# version 09
# baseline (speedup 1.0000x reference)
import contextlib
import numpy as np
import ml_dtypes
import concourse.bass as bass
import concourse.mybir as mybir
from concourse.bass_utils import run_bass_kernel_spmd

F32 = mybir.dt.float32
BF16 = mybir.dt.bfloat16
AF = mybir.ActivationFunctionType
OP = mybir.AluOpType
AX = mybir.AxisListType
BF = ml_dtypes.bfloat16

S = 2048
DM = 1024
NTI = 16
DK = 8
AH = 8
MH = 16
LAT = 512
NEXP = 8
EXPD = 2048

ENGS = ("sync", "gpsimd", "tensor", "vector", "scalar")


class Sched:
    def __init__(self, nc, es):
        self.nc = nc
        self.es = es
        self.q = {e: [] for e in ENGS}
        self.sem = {e: es.enter_context(nc.semaphore(f"sem_{e}")) for e in ENGS}
        self.cnt = {e: 0 for e in ENGS}
        self.streams = {}

    def stream(self, name, eng, ns=2):
        if name not in self.streams:
            sems = [self.es.enter_context(self.nc.semaphore(f"dma_{name}{i}"))
                    for i in range(ns)]
            self.streams[name] = [eng, sems, [0] * ns, 0]
        return name

    def dmas(self, name, build):
        eng, sems, cnts, idx = self.streams[name]
        s = sems[idx % len(sems)]
        i = idx % len(sems)
        if cnts[i] > 0:
            self.q[eng].append(lambda e, s=s, v=16 * cnts[i]: e.wait_ge(s, v))
        cnts[i] += 1
        self.streams[name][3] = idx + 1

        def fn(e, b=build, s=s):
            b(e).then_inc(s, 16)
        self.q[eng].append(fn)
        return (s, 16 * cnts[i])

    def quiesce(self, names, engines=ENGS):
        for n in names:
            if n in self.streams:
                for m in self.stream_done(n):
                    for e in engines:
                        self.wait(e, m)

    def stream_done(self, name):
        eng, sems, cnts, idx = self.streams[name]
        return [(sems[i], 16 * cnts[i]) for i in range(len(sems)) if cnts[i] > 0]

    def emit(self, eng, build, sig=False):
        if eng in ("vector", "scalar"):
            if self.cnt[eng] > 0:
                self.q[eng].append(
                    lambda e, s=self.sem[eng], v=self.cnt[eng]: e.wait_ge(s, v))
            sig = True
        if sig:
            self.cnt[eng] += 1
            sem = self.sem[eng]

            def fn(e, b=build, sem=sem):
                b(e).then_inc(sem, 1)
        else:
            def fn(e, b=build):
                b(e)
        self.q[eng].append(fn)

    def dma(self, eng, build):
        self.cnt[eng] += 16
        sem = self.sem[eng]

        def fn(e, b=build, sem=sem):
            b(e).then_inc(sem, 16)
        self.q[eng].append(fn)

    def certify(self, eng):
        self.wait(eng, self.mark(eng))
        self.emit(eng, lambda e: e.nop(nofuse=True), sig=True)

    def emits(self, eng, build):
        self.emit(eng, build, sig=True)
        self.wait(eng, self.mark(eng))

    def mark(self, eng):
        return (self.sem[eng], self.cnt[eng])

    def wait(self, eng, mark):
        sem, val = mark
        if val > 0:
            self.q[eng].append(lambda e, sem=sem, val=val: e.wait_ge(sem, val))

    def wait_all(self, eng, *marks):
        for m in marks:
            self.wait(eng, m)

    def barrier(self):
        for name, (eng, sems, cnts, idx) in self.streams.items():
            for m in self.stream_done(name):
                self.wait(eng, m)
        marks = {}
        for e in ENGS:
            self.wait(e, self.mark(e))
            self.emit(e, lambda en: en.nop(nofuse=True), sig=True)
            marks[e] = self.mark(e)
        for e in ENGS:
            for e2 in ENGS:
                if e2 != e:
                    self.wait(e, marks[e2])

    def replay(self, block):
        for name in ENGS:
            ops = self.q[name]
            getattr(block, name)(lambda engine, ops=ops: [fn(engine) for fn in ops])


def _r3(dram, p=128):
    return dram.ap().rearrange("(k p) n -> p k n", p=p)


class Ctx:
    pass


class PsHalf:

    def __init__(self, t, off):
        self.t, self.off = t, off

    def __getitem__(self, idx):
        p, f = idx
        lo = (f.start or 0) + self.off
        hi = (512 if f.stop is None else f.stop) + self.off
        return self.t[p, lo:hi]


def mmi(c, out_ap, lhsT_ap, rhs_ap, start, stop, sig=False):
    c.sch.emit("tensor",
               lambda t, o=out_ap, l=lhsT_ap, r=rhs_ap, st=start, sp=stop:
               t.matmul(o, l, r, start=st, stop=sp), sig=sig)


def tri(c, out_ap, in_ap, sig=False):
    c.sch.emit("tensor",
               lambda t, o=out_ap, i=in_ap: t.transpose(o, i, c.ident[:, :]), sig=sig)



def build_program(taps=(), upto=4, dbg=False, p2_heads=MH, p2_mub=True, p2_stage=9, h2mode=3):
    nc = bass.Bass("TRN2", target_bir_lowering=False)
    c = Ctx()
    c.nc = nc
    es = contextlib.ExitStack()
    c.es = es

    D = {}
    D["x"] = nc.dram_tensor("x", [S, DM], F32, kind="ExternalInput")
    for n, sh in [("aWq", [DM, DM]), ("aWk", [DM, DM]), ("aWv", [DM, DM]), ("aWo", [DM, DM]),
                  ("normw_tok", [128, DM]), ("g1b", [128, DM]), ("g2b", [128, DM]),
                  ("fWq", [DM, DM]), ("fWdkv", [DM, LAT]), ("fWuk", [LAT, DM]),
                  ("fWuv", [LAT, DM]), ("fWo", [DM, DM]), ("gate_w", [DM, NEXP]),
                  ("c_id", [128, 128]), ("c_triu", [128, 128]), ("c_ones", [128, 128])]:
        D[n] = nc.dram_tensor(n, sh, F32, kind="ExternalInput")
    for e in range(NEXP):
        D[f"w1_{e}"] = nc.dram_tensor(f"w1_{e}", [DM, EXPD], F32, kind="ExternalInput")
        D[f"w2_{e}"] = nc.dram_tensor(f"w2_{e}", [EXPD, DM], F32, kind="ExternalInput")
    D["xn_sp"] = nc.dram_tensor("xn_sp", [S, DM], F32, kind="Internal")
    D["s_sp"] = nc.dram_tensor("s_sp", [S, DM], F32, kind="Internal")
    D["h1_sp"] = nc.dram_tensor("h1_sp", [S, DM], F32, kind="Internal")
    D["y"] = nc.dram_tensor("y", [S, DM], F32, kind="ExternalOutput")
    c.taps = {}
    for t in taps:
        c.taps[t] = nc.dram_tensor("tap_" + t, [S, DM], F32, kind="ExternalOutput")
    c.p2_heads = p2_heads
    c.p2_mub = p2_mub
    c.p2_stage = p2_stage
    c.h2mode = h2mode
    c.dbg = {}
    if dbg:
        for n, sh in [("qT", [128, AH * S]), ("kT", [128, AH * S]),
                      ("vt", [S, DM]), ("kt", [S, DM]), ("o2", [S, DM]),
                      ("qz", [S, AH]), ("scm", [S, AH * 128]), ("st", [S, AH * 128]),
                      ("car", [S, AH]), ("sout", [S, DM])]:
            c.dbg[n] = nc.dram_tensor("dbg_" + n, sh, F32, kind="ExternalOutput")
    c.D = D

    with es:
        sch = Sched(nc, es)
        c.sch = sch

        def sb(name, shape, dt, side="right", stack=None):
            return (stack or es).enter_context(nc.sbuf_tensor(name, shape, dt, side=side))
        c.sb = sb

        c.ident = sb("ident", [128, 128], BF16)
        c.triu = sb("triu", [128, 128], BF16)
        c.onesb = sb("onesb", [128, 128], BF16)
        c.normw = sb("normw", [128, DM], F32)
        c.g1b = sb("g1b_s", [128, DM], F32)
        c.g2b = sb("g2b_s", [128, DM], F32)

        c.pA = es.enter_context(nc.psum_tensor("pA", [128, 1024], F32))
        c.pB = es.enter_context(nc.psum_tensor("pB", [128, 1024], F32))
        ps4 = es.enter_context(nc.psum_tensor("ps4", [128, 512], F32))
        ps5 = es.enter_context(nc.psum_tensor("ps5", [128, 512], F32))
        c.ps = [PsHalf(c.pA, 0), PsHalf(c.pA, 512), PsHalf(c.pB, 0), PsHalf(c.pB, 512),
                ps4, ps5]
        c.pw = es.enter_context(nc.psum_tensor("pw", [128, 512], F32))
        c.pt = es.enter_context(nc.psum_tensor("pt", [128, 1024], BF16))

        sch.stream("cg", "gpsimd", 1)
        sch.stream("cs", "sync", 1)
        for dst, src in [(c.ident, D["c_id"]), (c.triu, D["c_triu"]), (c.onesb, D["c_ones"])]:
            c.m_const_g = sch.dmas("cg", lambda g, d=dst, s=src: g.dma_start(d[:, :], s.ap()))
        for dst, src in [(c.normw, D["normw_tok"]), (c.g1b, D["g1b"]), (c.g2b, D["g2b"])]:
            c.m_const_s = sch.dmas("cs", lambda s, d=dst, r=src: s.dma_start(d[:, :], r.ap()))
        for e in ("tensor", "vector", "scalar"):
            sch.wait(e, c.m_const_g)
            sch.wait(e, c.m_const_s)

        c.es01 = contextlib.ExitStack()
        p0_ln(c)
        if upto >= 1:
            p1_attn(c)
        c.es01.close()
        if upto >= 2:
            p2_mla(c)
        if upto >= 3:
            p3_moe(c)
        if upto >= 4:
            p4_final(c)
        finish(c)

        with nc.Block() as block:
            sch.replay(block)
    return nc, c



def p0_ln(c):
    sch, sb = c.sch, c.sb
    pes = contextlib.ExitStack()
    sb = lambda n, sh, dt, side="left": c.sb(n, sh, dt, side=side, stack=pes)
    c.xnT = c.sb("xnT", [128, DK * S], BF16, side="left", stack=c.es01)
    xf = [sb(f"p0x{i}", [128, DM], F32, side="left") for i in range(2)]
    xnf = [sb(f"p0xn{i}", [128, DM], F32, side="left") for i in range(2)]
    xnb = [sb(f"p0xnb{i}", [128, DM], BF16, side="left") for i in range(2)]
    h1b = [sb(f"p0h1b{i}", [128, DM], BF16, side="left") for i in range(2)]
    sqs = sb("p0sqs", [128, DM], BF16, side="left")
    s1 = sb("p0s1", [128, 1], F32, side="left")
    mm2 = sb("p0mm2", [128, 1], F32, side="left")
    nm = sb("p0nm", [128, 1], F32, side="left")
    iv = sb("p0iv", [128, 1], F32, side="left")
    rstd = sb("p0rstd", [128, 1], F32, side="left")
    ssq = sb("p0ssq", [128, 1], F32, side="left")
    ms = sb("p0ms", [128, 1], F32, side="left")
    r2 = sb("p0r2", [128, 1], F32, side="left")

    xnTv = c.xnT[:, :].rearrange("p (k t) -> p k t", t=S)
    ptv = c.pt[:, :].rearrange("p (k t) -> p k t", t=128)

    sch.stream("x0", "sync", 3)
    sch.stream("xnsp", "sync", 2)
    sch.stream("h1sp", "gpsimd", 2)
    free_m = [None, None]
    evac_m = [None, None]
    h1sp_m = [None, None]
    spill_m = [None, None]
    for i in range(NTI):
        par = i % 2
        if free_m[par] is not None:
            sch.wait("sync", free_m[par])
        if h1sp_m[par] is not None:
            sch.wait("vector", h1sp_m[par])
        if spill_m[par] is not None:
            sch.wait("vector", spill_m[par])
        m_in = sch.dmas("x0", lambda s, d=xf[par], i=i:
                        s.dma_start(d[:, :], c.D["x"].ap()[i * 128:(i + 1) * 128, :]))

        sch.wait("vector", m_in)
        sch.wait("scalar", m_in)
        sch.emits("vector", lambda e, x=xf[par]: e.reduce_sum(s1[:, :], x[:, :], axis=AX.X))
        sch.emits("vector", lambda e: e.tensor_scalar_mul(nm[:, :], s1[:, :], -1.0 / DM))
        sch.emit("scalar", lambda e, x=xf[par]:
                 e.activation(sqs[:, :], x[:, :], AF.Square, accum_out=ssq[:, :]), sig=True)
        sch.wait("vector", sch.mark("scalar"))
        sch.emits("vector", lambda e: e.tensor_mul(mm2[:, :], nm[:, :], nm[:, :]))
        sch.emits("vector", lambda e: e.tensor_scalar(iv[:, :], ssq[:, :], 1.0 / DM, 1e-5,
                                                      OP.mult, OP.add))
        sch.emits("vector", lambda e: e.tensor_sub(iv[:, :], iv[:, :], mm2[:, :]))
        sch.emits("vector", lambda e: e.reciprocal(iv[:, :], iv[:, :]))
        sch.wait("scalar", sch.mark("vector"))
        sch.emit("scalar", lambda e: e.activation(rstd[:, :], iv[:, :], AF.Sqrt), sig=True)
        sch.wait("vector", sch.mark("scalar"))
        sch.emit("vector", lambda e, x=xf[par], xn=xnf[par]:
                 e.tensor_scalar(xn[:, :], x[:, :], nm[:, :], rstd[:, :], OP.add, OP.mult))
        sch.emit("vector", lambda e, xn=xnf[par], xb=xnb[par]:
                 e.tensor_copy(xb[:, :], xn[:, :]), sig=True)
        m_xn = sch.mark("vector")

        sch.wait("sync", m_xn)
        m_xnsp = sch.dmas("xnsp", lambda s, xn=xnf[par], i=i:
                          s.dma_start(c.D["xn_sp"].ap()[i * 128:(i + 1) * 128, :], xn[:, :]))
        spill_m[par] = m_xnsp

        sch.wait("scalar", m_xn)
        sch.emit("scalar", lambda e, xn=xnf[par]:
                 e.activation(sqs[:, :], xn[:, :], AF.Square, accum_out=ssq[:, :]), sig=True)
        sch.wait("vector", sch.mark("scalar"))
        sch.emits("vector", lambda e: e.tensor_scalar(ms[:, :], ssq[:, :], 1.0 / DM, 1e-6,
                                                      OP.mult, OP.add))
        sch.emits("vector", lambda e: e.reciprocal(ms[:, :], ms[:, :]))
        sch.wait("scalar", sch.mark("vector"))
        sch.emit("scalar", lambda e: e.activation(r2[:, :], ms[:, :], AF.Sqrt), sig=True)
        sch.wait("vector", sch.mark("scalar"))
        sch.emit("vector", lambda e, xn=xnf[par], hb=h1b[par]:
                 e.scalar_tensor_tensor(hb[:, :], xn[:, :], r2[:, :], c.g1b[:, :],
                                        OP.mult, OP.mult), sig=True)
        m_h1 = sch.mark("vector")
        free_m[par] = m_h1

        sch.wait("gpsimd", m_h1)
        h1sp_m[par] = sch.dmas("h1sp", lambda g, hb=h1b[par], i=i:
                               g.dma_start(c.D["h1_sp"].ap()[i * 128:(i + 1) * 128, :],
                                           hb[:, :]))

        sch.wait("tensor", m_xn)
        if evac_m[par] is not None:
            sch.wait("tensor", evac_m[par])
        for k in range(DK):
            tri(c, c.pt[:, 128 * k:128 * (k + 1)], xnb[par][:, 128 * k:128 * (k + 1)],
                sig=(k == DK - 1))
        sch.wait("scalar", sch.mark("tensor"))
        sch.emit("scalar", lambda e, i=i: e.copy(xnTv[:, :, i * 128:(i + 1) * 128], ptv),
                 sig=True)
        sch.certify("scalar")
        evac_m[par] = sch.mark("scalar")
    sch.barrier()
    pes.close()



def p1_attn(c):
    sch = c.sch
    sch.stream("wl", "gpsimd", 2)
    sch.stream("ssp", "sync", 2)
    pes = contextlib.ExitStack()
    sb = lambda n, sh, dt, side="left": c.sb(n, sh, dt, side=side, stack=pes)
    wnat = sb("p1wnat", [128, DK * DM], BF16, side="left")
    wqT = sb("p1wqT", [128, DK * DM], BF16, side="left")
    qT = sb("p1qT", [128, AH * S], BF16, side="left")
    kT = sb("p1kT", [128, AH * S], BF16, side="left")
    wvT = sb("p1wvT", [128, DK * DM], BF16, side="left")
    woT = sb("p1woT", [128, DK * DM], BF16, side="left")

    ptv = c.pt[:, :].rearrange("p (k t) -> p k t", t=128)
    xv = c.xnT[:, :].rearrange("p (k t) -> p k t", t=S)

    def load_and_transpose(dstT, dram):
        sch.wait("gpsimd", sch.mark("tensor"))
        m_ld = sch.dmas("wl", lambda g: g.dma_start(
            wnat[:, :].rearrange("p (k n) -> p k n", n=DM), _r3(dram)))
        sch.wait("tensor", m_ld)
        dv = dstT[:, :].rearrange("p (k n) -> p k n", n=DM)
        for me in range(DK):
            sch.wait("tensor", sch.mark("scalar"))
            for kd in range(DK):
                tri(c, c.pt[:, 128 * kd:128 * (kd + 1)],
                    wnat[:, me * DM + 128 * kd:me * DM + 128 * (kd + 1)], sig=(kd == DK - 1))
            sch.wait("scalar", sch.mark("tensor"))
            sch.emit("scalar", lambda e, d=dv[:, :, me * 128:(me + 1) * 128]:
                     e.copy(d, ptv), sig=True)
        sch.certify("scalar")
        sch.wait("tensor", sch.mark("scalar"))

    def projB(dstv, wv, kk, n_mt):
        bmark = [None, None]
        for m0 in range(n_mt):
            for sl in range(4):
                par = sl % 2
                pb = c.ps[par]
                if bmark[par] is not None:
                    sch.wait("tensor", bmark[par])
                for k in range(kk):
                    mmi(c, pb[:, 0:512], wv[:, k, m0 * 128:(m0 + 1) * 128],
                        xv[:, k, sl * 512:(sl + 1) * 512], k == 0, k == kk - 1,
                        sig=(k == kk - 1))
                sch.wait("scalar", sch.mark("tensor"))
                sch.emit("scalar", lambda e, d=dstv[:, m0, sl * 512:(sl + 1) * 512], p=pb:
                         e.copy(d, p[:, 0:512]), sig=True)
                bmark[par] = sch.mark("scalar")
        sch.certify("scalar")
        sch.wait("tensor", sch.mark("scalar"))

    load_and_transpose(wqT, c.D["aWq"])
    qTv = qT[:, :].rearrange("p (h t) -> p h t", t=S)
    projB(qTv, wqT[:, :].rearrange("p (k n) -> p k n", n=DM), DK, AH)
    load_and_transpose(wqT, c.D["aWk"])
    kTv = kT[:, :].rearrange("p (h t) -> p h t", t=S)
    projB(kTv, wqT[:, :].rearrange("p (k n) -> p k n", n=DM), DK, AH)
    if c.dbg:
        sch.stream("dbg", "gpsimd", 1)
        sch.wait("gpsimd", sch.mark("scalar"))
        sch.dmas("dbg", lambda g: g.dma_start(c.dbg["qT"].ap(), qT[:, :]))
        sch.dmas("dbg", lambda g: g.dma_start(c.dbg["kT"].ap(), kT[:, :]))
    load_and_transpose(wvT, c.D["aWv"])
    load_and_transpose(woT, c.D["aWo"])
    wvv = wvT[:, :].rearrange("p (k n) -> p k n", n=DM)
    wov = woT[:, :].rearrange("p (k n) -> p k n", n=DM)

    state_f = sb("p1state", [128, AH * 128], F32, side="left")
    state_b = sb("p1stateb", [128, AH * 128], BF16, side="left")
    carry_f = sb("p1carry", [128, AH], F32, side="left")
    carry_b = sb("p1carryb", [128, AH], BF16, side="left")
    dcar = sb("p1dcar", [128, AH], F32, side="left")
    vtok = [sb(f"p1vt{i}", [128, DM], BF16, side="left") for i in range(2)]
    ktok = [sb(f"p1kt{i}", [128, DM], BF16, side="left") for i in range(2)]
    scm = [sb(f"p1scm{i}", [128, AH * 128], BF16, side="left") for i in range(2)]
    o2b = sb("p1o2b", [128, DM], BF16, side="left")
    oTc = sb("p1oTc", [128, DM], BF16, side="left")
    o1f = sb("p1o1f", [128, DM], F32, side="left")
    sqf = sb("p1sqf", [128, DM], F32, side="left")
    qzf = sb("p1qz", [128, AH], F32, side="left")
    rdn = sb("p1rdn", [128, AH], F32, side="left")
    ssq8 = sb("p1ssq8", [128, AH], F32, side="left")
    rst8 = sb("p1rst8", [128, AH], F32, side="left")
    sout = sb("p1sout", [128, DM], F32, side="left")

    for t_ in (state_f, state_b, carry_f, carry_b):
        sch.emit("vector", lambda e, t=t_: e.memset(t[:, :], 0.0), sig=(t_ is carry_b))
    sch.certify("vector")
    sch.wait("tensor", sch.mark("vector"))
    m_o1read = m_qzread = m_ots_p = m_state_p = m_sout_p = m_ot_p = m_sp_p = m_ssp_p = None

    for ci in range(NTI):
        par = ci % 2
        t0, t1 = ci * 128, (ci + 1) * 128

        if m_o1read is not None:
            sch.wait("tensor", m_o1read)
        for sl in range(2):
            for k in range(DK):
                mmi(c, c.ps[sl][:, 0:512], xv[:, k, t0:t1], wvv[:, k, sl * 512:(sl + 1) * 512],
                    k == 0, k == DK - 1, sig=(k == DK - 1))
        sch.wait("scalar", sch.mark("tensor"))
        for sl in range(2):
            sch.emit("scalar", lambda e, sl=sl, d=vtok[par]:
                     e.copy(d[:, sl * 512:(sl + 1) * 512], c.ps[sl][:, 0:512]), sig=(sl == 1))
        sch.certify("scalar")
        m_vs = sch.mark("scalar")

        if m_ots_p is not None:
            sch.wait("tensor", m_ots_p)
        for h in range(AH):
            tri(c, c.pt[:, 128 * h:128 * (h + 1)], kTv[:, h, t0:t1], sig=(h == AH - 1))
        sch.wait("scalar", sch.mark("tensor"))
        sch.emit("scalar", lambda e, d=ktok[par]: e.copy(d[:, :], c.pt[:, :]), sig=True)
        sch.certify("scalar")
        m_ks = sch.mark("scalar")
        if c.dbg:
            sch.wait("gpsimd", m_vs)
            sch.dmas("dbg", lambda g, ci=ci, d=vtok[par]:
                     g.dma_start(c.dbg["vt"].ap()[ci * 128:(ci + 1) * 128, :], d[:, :]))
            sch.wait("gpsimd", m_ks)
            sch.dmas("dbg", lambda g, ci=ci, d=ktok[par]:
                     g.dma_start(c.dbg["kt"].ap()[ci * 128:(ci + 1) * 128, :], d[:, :]))

        if m_sout_p is not None:
            sch.wait("tensor", m_sout_p)
        for h in range(AH):
            pb = c.ps[2 + h // 4]
            mmi(c, pb[:, 128 * (h % 4):128 * (h % 4 + 1)], kTv[:, h, t0:t1], qTv[:, h, t0:t1],
                True, True, sig=(h == AH - 1))
        sch.wait("vector", sch.mark("tensor"))
        for h in range(AH):
            pb = c.ps[2 + h // 4]
            sch.emit("vector", lambda e, h=h, pb=pb, d=scm[par]:
                     e.tensor_tensor(d[:, 128 * h:128 * (h + 1)],
                                     pb[:, 128 * (h % 4):128 * (h % 4 + 1)],
                                     c.triu[:, :], OP.mult), sig=(h == AH - 1))
        sch.certify("vector")
        m_scm = sch.mark("vector")
        if c.dbg:
            sch.wait("gpsimd", m_scm)
            sch.dmas("dbg", lambda g, ci=ci, d=scm[par]:
                     g.dma_start(c.dbg["scm"].ap()[ci * 128:(ci + 1) * 128, :], d[:, :]))
            sch.dmas("dbg", lambda g, ci=ci:
                     g.dma_start(c.dbg["car"].ap()[ci * 128:(ci + 1) * 128, 0:AH],
                                 carry_b[:, :]))

        sch.wait("tensor", m_scm)
        sch.wait("tensor", m_vs)
        sch.wait("tensor", m_ks)
        if m_qzread is not None:
            sch.wait("tensor", m_qzread)
        if m_state_p is not None:
            sch.wait("tensor", m_state_p)
        for h in range(AH):
            ob = c.ps[h // 4][:, 128 * (h % 4):128 * (h % 4 + 1)]
            qb = c.pw[:, h:h + 1]
            if ci > 0:
                mmi(c, ob, qTv[:, h, t0:t1], state_b[:, 128 * h:128 * (h + 1)], True, False)
                mmi(c, qb, qTv[:, h, t0:t1], carry_b[:, h:h + 1], True, False)
                mmi(c, ob, scm[par][:, 128 * h:128 * (h + 1)],
                    vtok[par][:, 128 * h:128 * (h + 1)], False, True)
                mmi(c, qb, scm[par][:, 128 * h:128 * (h + 1)], c.onesb[:, 0:1], False, True,
                    sig=(h == AH - 1))
            else:
                mmi(c, ob, scm[par][:, 128 * h:128 * (h + 1)],
                    vtok[par][:, 128 * h:128 * (h + 1)], True, True)
                mmi(c, qb, scm[par][:, 128 * h:128 * (h + 1)], c.onesb[:, 0:1], True, True,
                    sig=(h == AH - 1))
        m_o = sch.mark("tensor")

        if m_state_p is not None:
            sch.wait("tensor", m_state_p)
        for h in range(AH):
            pb = c.ps[4 + h // 4]
            mmi(c, pb[:, 128 * (h % 4):128 * (h % 4 + 1)], ktok[par][:, 128 * h:128 * (h + 1)],
                vtok[par][:, 128 * h:128 * (h + 1)], True, True, sig=(h == AH - 1))
        m_sd = sch.mark("tensor")
        if ci < NTI - 1:
            sch.wait("vector", m_sd)
            for half in range(2):
                sch.emit("vector", lambda e, half=half:
                         e.tensor_add(state_f[:, half * 512:(half + 1) * 512],
                                      state_f[:, half * 512:(half + 1) * 512],
                                      c.ps[4 + half][:, 0:512]))
            sch.emit("vector", lambda e: e.tensor_copy(state_b[:, :], state_f[:, :]),
                     sig=True)
            sch.wait("scalar", sch.mark("vector"))
            sch.emits("vector", lambda e, src=kTv[:, :, t0:t1]:
                      e.reduce_sum(dcar[:, :], src, axis=AX.X))
            sch.emits("vector", lambda e: e.tensor_add(carry_f[:, :], carry_f[:, :], dcar[:, :]))
            sch.emit("vector", lambda e: e.tensor_copy(carry_b[:, :], carry_f[:, :]), sig=True)
            sch.certify("vector")
        m_state = sch.mark("vector")
        m_state_p = m_state

        sch.wait("vector", m_o)
        sch.emits("vector", lambda e: e.tensor_copy(qzf[:, :], c.pw[:, 0:AH]))
        m_qzread = sch.mark("vector")
        sch.emits("vector", lambda e: e.scalar_tensor_tensor(rdn[:, :], qzf[:, :], -1.0,
                                                             qzf[:, :], OP.mult, OP.max))
        sch.emits("vector", lambda e: e.tensor_scalar_max(rdn[:, :], rdn[:, :], 1.0))
        sch.emits("vector", lambda e: e.reciprocal(rdn[:, :], rdn[:, :]))
        for half in range(2):
            sch.emit("vector", lambda e, half=half:
                     e.tensor_tensor(
                         o1f[:, half * 512:(half + 1) * 512].rearrange("p (h t) -> p h t", t=128),
                         c.ps[half][:, 0:512].rearrange("p (h t) -> p h t", t=128),
                         rdn[:, half * 4:half * 4 + 4].to_broadcast([128, 4, 128]),
                         OP.mult), sig=(half == 1))
        m_o1read = sch.mark("vector")
        sch.wait("scalar", sch.mark("vector"))
        sch.emit("scalar", lambda e: e.activation(sqf[:, :], o1f[:, :], AF.Square), sig=True)
        sch.wait("vector", sch.mark("scalar"))
        sch.emits("vector", lambda e: e.reduce_sum(
            ssq8[:, :], sqf[:, :].rearrange("p (h t) -> p h t", t=128), axis=AX.X))
        sch.emits("vector", lambda e: e.tensor_scalar(ssq8[:, :], ssq8[:, :], 1.0 / 128, 1e-5,
                                                      OP.mult, OP.add))
        sch.emits("vector", lambda e: e.reciprocal(ssq8[:, :], ssq8[:, :]))
        sch.wait("scalar", sch.mark("vector"))
        sch.emit("scalar", lambda e: e.activation(rst8[:, :], ssq8[:, :], AF.Sqrt), sig=True)
        sch.wait("vector", sch.mark("scalar"))
        sch.emit("vector", lambda e:
                 e.tensor_tensor(o1f[:, :].rearrange("p (h t) -> p h t", t=128),
                                 o1f[:, :].rearrange("p (h t) -> p h t", t=128),
                                 rst8[:, 0:AH].to_broadcast([128, AH, 128]), OP.mult))
        if m_ot_p is not None:
            sch.wait("vector", m_ot_p)
        sch.emit("vector", lambda e: e.tensor_tensor(o2b[:, :], o1f[:, :], c.normw[:, :],
                                                     OP.mult), sig=True)
        sch.certify("vector")
        m_o2 = sch.mark("vector")
        if c.dbg:
            sch.wait("gpsimd", m_o2)
            sch.dmas("dbg", lambda g, ci=ci:
                     g.dma_start(c.dbg["qz"].ap()[ci * 128:(ci + 1) * 128, :], qzf[:, :]))
            sch.dmas("dbg", lambda g, ci=ci:
                     g.dma_start(c.dbg["o2"].ap()[ci * 128:(ci + 1) * 128, :], o2b[:, :]))
            m_dbg = sch.dmas("dbg", lambda g, ci=ci:
                             g.dma_start(c.dbg["st"].ap()[ci * 128:(ci + 1) * 128, :],
                                         state_b[:, :]))
            sch.wait("vector", m_dbg)
            sch.wait("tensor", m_dbg)

        sch.wait("tensor", m_o2)
        for k in range(DK):
            tri(c, c.pt[:, 128 * k:128 * (k + 1)], o2b[:, 128 * k:128 * (k + 1)],
                sig=(k == DK - 1))
        m_ot_p = sch.mark("tensor")
        if m_sp_p is not None:
            sch.wait("scalar", m_sp_p)
        sch.wait("scalar", sch.mark("tensor"))
        sch.emit("scalar", lambda e: e.copy(oTc[:, :], c.pt[:, :]), sig=True)
        sch.certify("scalar")
        m_ots_p = sch.mark("scalar")
        sch.wait("tensor", m_ots_p)
        for sl in range(2):
            pb = c.ps[2 + sl]
            for k in range(DK):
                mmi(c, pb[:, 0:512], oTc[:, 128 * k:128 * (k + 1)],
                    wov[:, k, sl * 512:(sl + 1) * 512], k == 0, k == DK - 1,
                    sig=(k == DK - 1))
        m_sp_p = sch.mark("tensor")
        sch.wait("scalar", m_sp_p)
        if m_ssp_p is not None:
            sch.wait("scalar", m_ssp_p)
        for sl in range(2):
            sch.emit("scalar", lambda e, sl=sl:
                     e.copy(sout[:, sl * 512:(sl + 1) * 512], c.ps[2 + sl][:, 0:512]),
                     sig=(sl == 1))
        sch.certify("scalar")
        m_sout = sch.mark("scalar")
        m_sout_p = m_sout
        sch.wait("sync", m_sout)
        m_ssp = sch.dmas("ssp", lambda s, ci=ci:
                         s.dma_start(c.D["s_sp"].ap()[ci * 128:(ci + 1) * 128, :], sout[:, :]))
        m_ssp_p = m_ssp
        if c.dbg:
            sch.wait("gpsimd", m_sout)
            m_dbg2 = sch.dmas("dbg", lambda g, ci=ci:
                              g.dma_start(c.dbg["sout"].ap()[ci * 128:(ci + 1) * 128, :],
                                          sout[:, :]))
            sch.wait("vector", m_dbg2)

    sch.barrier()
    pes.close()


def tap_copy_dram(c, src, dst, nm):
    sch = c.sch
    pes = contextlib.ExitStack()
    buf = c.sb("tap_" + nm, [128, DM], F32, side="left", stack=pes)
    st = sch.stream("tap_" + nm, "sync", 1)
    for i in range(NTI):
        sch.dmas(st, lambda s, i=i: s.dma_start(buf[:, :], src.ap()[i * 128:(i + 1) * 128, :]))
        sch.dmas(st, lambda s, i=i: s.dma_start(dst.ap()[i * 128:(i + 1) * 128, :], buf[:, :]))
    pes.close()



def p2_mla(c):
    sch = c.sch
    pesA = contextlib.ExitStack()
    pesQ = contextlib.ExitStack()
    pesH = contextlib.ExitStack()
    sb = lambda n, sh, dt, side="left", stack=None: c.sb(n, sh, dt, side=side, stack=stack or pesA)

    wbuf = sb("p2w", [128, DK * DM], BF16)
    omla = sb("p2omla", [128, DK * S], BF16)
    omv = omla[:, :].rearrange("p (m t) -> p m t", t=S)
    qT = sb("p2qT", [128, DK * S], BF16, stack=pesQ)
    cT = sb("p2cT", [128, 4 * S], BF16, stack=pesQ)
    h1T = sb("p2h1T", [128, DK * S], BF16, stack=pesH)

    ptv = c.pt[:, :].rearrange("p (k t) -> p k t", t=128)
    h1v = h1T[:, :].rearrange("p (k t) -> p k t", t=S)
    qv = qT[:, :].rearrange("p (m t) -> p m t", t=S)
    cv = cT[:, :].rearrange("p (m t) -> p m t", t=S)

    sch.stream("h1ld", "gpsimd", 2)
    sch.stream("wl2", "gpsimd", 2)
    sch.stream("xnr", "sync", 2)
    h1tok = [sb(f"p2h1t{i}", [128, DM], BF16, stack=pesH) for i in range(2)]
    prev = [None, None]
    for i in range(NTI):
        par = i % 2
        if prev[par] is not None:
            sch.wait("gpsimd", prev[par])
        m_h1ld = sch.dmas("h1ld", lambda g, i=i, par=par:
                          g.dma_start(h1tok[par][:, :],
                                      c.D["h1_sp"].ap()[i * 128:(i + 1) * 128, :]))
        sch.wait("tensor", m_h1ld)
        sch.wait("tensor", sch.mark("scalar"))
        for k in range(DK):
            tri(c, c.pt[:, 128 * k:128 * (k + 1)], h1tok[par][:, 128 * k:128 * (k + 1)],
                sig=(k == DK - 1))
        sch.wait("scalar", sch.mark("tensor"))
        sch.emit("scalar", lambda e, i=i:
                 e.copy(h1v[:, :, i * 128:(i + 1) * 128], ptv), sig=True)
        sch.certify("scalar")
        prev[par] = sch.mark("scalar")
    sch.wait("tensor", sch.mark("scalar"))

    def load_native(dram, ktiles, ncols):
        sch.wait("gpsimd", sch.mark("tensor"))
        m_ld = sch.dmas("wl2", lambda g, kt=ktiles, nn=ncols: g.dma_start(
            wbuf[:, 0:kt * nn].rearrange("p (k n) -> p k n", n=nn), _r3(dram)))
        sch.wait("tensor", m_ld)
        return wbuf[:, 0:ktiles * ncols].rearrange("p (k n) -> p k n", n=ncols)

    def projB(dstv, wv, srcv, kk, n_mt, scale=None):
        bmark = [None, None]
        for m0 in range(n_mt):
            for sl in range(4):
                par = sl % 2
                pb = c.ps[par]
                if bmark[par] is not None:
                    sch.wait("tensor", bmark[par])
                for k in range(kk):
                    mmi(c, pb[:, 0:512], wv[:, k, m0 * 128:(m0 + 1) * 128],
                        srcv[:, k, sl * 512:(sl + 1) * 512], k == 0, k == kk - 1,
                        sig=(k == kk - 1))
                sch.wait("scalar", sch.mark("tensor"))
                if scale is None:
                    sch.emit("scalar", lambda e, d=dstv[:, m0, sl * 512:(sl + 1) * 512], p=pb:
                             e.copy(d, p[:, 0:512]), sig=True)
                else:
                    sch.emit("scalar", lambda e, d=dstv[:, m0, sl * 512:(sl + 1) * 512], p=pb,
                             sc=scale: e.activation(d, p[:, 0:512], AF.Copy, scale=sc), sig=True)
                bmark[par] = sch.mark("scalar")
        sch.certify("scalar")
        sch.wait("tensor", sch.mark("scalar"))

    if c.p2_stage < 2:
        pesH.close()
        pesQ.close()
        c.acc = c.sb("acc", [128, NTI * DM], F32)
        sch.barrier()
        pesA.close()
        return
    wq = load_native(c.D["fWq"], DK, DM)
    projB(qv, wq, h1v, DK, DK, scale=0.125)
    if c.p2_stage < 3:
        pesH.close()
        pesQ.close()
        c.acc = c.sb("acc", [128, NTI * DM], F32)
        sch.barrier()
        pesA.close()
        return
    wdkv = load_native(c.D["fWdkv"], DK, LAT)
    projB(cv, wdkv, h1v, DK, 4)
    sch.quiesce(["h1ld"])
    sch.wait("vector", sch.mark("scalar"))
    sch.wait("gpsimd", sch.mark("scalar"))
    pesH.close()
    kT = sb("p2kT", [128, DK * S], BF16, stack=pesQ)
    vaug = sb("p2vaug", [128, NTI * MH * 65], BF16, stack=pesQ)
    kv = kT[:, :].rearrange("p (m t) -> p m t", t=S)
    vv = vaug[:, :].rearrange("p (i h n) -> p i h n", h=MH, n=65)
    if c.p2_stage < 4:
        pesQ.close()
        c.acc = c.sb("acc", [128, NTI * DM], F32)
        sch.barrier()
        pesA.close()
        return
    wuk = load_native(c.D["fWuk"], 4, DM)
    projB(kv, wuk, cv, 4, DK)
    if c.p2_stage < 5:
        pesQ.close()
        c.acc = c.sb("acc", [128, NTI * DM], F32)
        sch.barrier()
        pesA.close()
        return
    wuv = load_native(c.D["fWuv"], 4, DM)
    sch.emit("vector", lambda e: e.memset(vv[:, :, :, 64:65], 1.0), sig=True)
    sch.wait("tensor", sch.mark("vector"))
    vmark = [None, None]
    for i in range(NTI):
        for sl in range(2):
            par = sl % 2
            pb = c.ps[par]
            if vmark[par] is not None:
                sch.wait("tensor", vmark[par])
            for k in range(4):
                mmi(c, pb[:, 0:512], cv[:, k, i * 128:(i + 1) * 128],
                    wuv[:, k, sl * 512:(sl + 1) * 512], k == 0, k == 3, sig=(k == 3))
            sch.wait("vector", sch.mark("tensor"))
            dst = vv[:, i, sl * 8:(sl + 1) * 8, 0:64]
            sch.emit("vector", lambda e, d=dst, p=pb:
                     e.tensor_copy(d, p[:, 0:512].rearrange("p (h n) -> p h n", n=64)),
                     sig=True)
            vmark[par] = sch.mark("vector")
    sch.certify("vector")
    sch.wait("tensor", sch.mark("vector"))

    sq2 = sb("p2sq2", [128, 2048], BF16, stack=pesQ)
    qkmax = sb("p2qkmax", [1, 2 * DK], F32, stack=pesQ)
    tmp1 = sb("p2tmp1", [1, 1], F32, stack=pesQ)
    mubb = sb("p2mubb", [1, DK], BF16, stack=pesQ)
    nmub = sb("p2nmub", [128, DK], F32, stack=pesQ)
    sch.emit("vector", lambda e: e.memset(qkmax[:, :], 0.0), sig=True)
    sch.certify("vector")
    sch.wait("scalar", sch.mark("vector"))
    if not c.p2_mub:
        sch.emits("vector", lambda e: e.memset(nmub[:, :], 0.0))
        sch.certify("vector")
        sch.wait("tensor", sch.mark("vector"))
        sch.wait("scalar", sch.mark("vector"))
    mubmark = None
    for which, src in (((0, qv), (1, kv)) if c.p2_mub else ()):
        for m0 in range(DK):
            sch.wait("scalar", sch.mark("vector"))
            for sl in range(4):
                sch.emit("scalar", lambda e, s=src[:, m0, sl * 512:(sl + 1) * 512], sl=sl:
                         e.activation(sq2[:, sl * 512:(sl + 1) * 512], s, AF.Square),
                         sig=True)
            sch.certify("scalar")
            if mubmark is not None:
                sch.wait("tensor", mubmark)
            sch.wait("tensor", sch.mark("scalar"))
            for sl in range(4):
                PS = c.pA if sl < 2 else c.pB
                mmi(c, PS[0:1, (sl % 2) * 512:(sl % 2) * 512 + 512], c.onesb[:, 0:1],
                    sq2[:, sl * 512:(sl + 1) * 512], True, True, sig=(sl == 3))
            sch.wait("vector", sch.mark("tensor"))
            col = which * DK + m0
            sch.emits("vector", lambda e: e.reduce_max(tmp1[:, :], c.pA[0:1, 0:1024],
                                                       axis=AX.X))
            sch.emits("vector", lambda e, col=col:
                      e.tensor_max(qkmax[:, col:col + 1], qkmax[:, col:col + 1],
                                   tmp1[:, :]))
            sch.emits("vector", lambda e: e.reduce_max(tmp1[:, :], c.pB[0:1, 0:1024],
                                                       axis=AX.X))
            sch.emit("vector", lambda e, col=col:
                     e.tensor_max(qkmax[:, col:col + 1], qkmax[:, col:col + 1],
                                  tmp1[:, :]), sig=True)
            sch.certify("vector")
            mubmark = sch.mark("vector")
    if c.p2_mub:
        sch.emits("vector", lambda e: e.tensor_mul(qkmax[:, 0:DK], qkmax[:, 0:DK],
                                                   qkmax[:, DK:2 * DK]))
        sch.wait("scalar", sch.mark("vector"))
        sch.emit("scalar", lambda e: e.activation(qkmax[:, 0:DK], qkmax[:, 0:DK], AF.Sqrt),
                 sig=True)
        sch.wait("vector", sch.mark("scalar"))
        sch.emit("vector", lambda e: e.tensor_copy(mubb[:, :], qkmax[:, 0:DK]), sig=True)
        sch.certify("vector")
        sch.wait("tensor", sch.mark("vector"))
        mmi(c, c.pw[:, 0:DK], c.onesb[0:1, :], mubb[:, :], True, True, sig=True)
        sch.wait("scalar", sch.mark("tensor"))
        sch.emit("scalar", lambda e: e.activation(nmub[:, :], c.pw[:, 0:DK], AF.Copy,
                                                  scale=-1.0), sig=True)
        sch.certify("scalar")
        sch.wait("tensor", sch.mark("scalar"))
        sch.wait("vector", sch.mark("scalar"))

    pbuf = [sb(f"p2pb{i}", [128, 1024], BF16, stack=pesQ) for i in range(2)]
    rlb = sb("p2rlb", [128, 512], BF16, stack=pesQ)
    rlf = sb("p2rlf", [128, 512], F32, stack=pesQ)
    lb = sb("p2lb", [128, 512], F32, stack=pesQ)
    scmark = [None, None]
    expmark = [None, None]
    pvgate = [None, None]
    pvfree = [None, None]
    bcfree = None
    hIi = [0]
    for h in range(c.p2_heads):
        m0 = h // 2
        pr = 64 * (h % 2)
        for I in range(4):
            qblk = qv[pr:pr + 64, m0, I * 512:(I + 1) * 512]
            npairs = 2 * (I + 1)
            hpar = hIi[0] % 2
            hIi[0] += 1
            oTb = c.pw if hpar == 0 else c.ps[5]

            def emit_sc(p_):
                par = p_ % 2
                PS = c.pA if par == 0 else c.pB
                if scmark[par] is not None:
                    sch.wait("tensor", scmark[par])
                for r in range(2):
                    j = 2 * p_ + r
                    mmi(c, PS[:, r * 512:(r + 1) * 512],
                        kv[pr:pr + 64, m0, j * 128:(j + 1) * 128],
                        qblk, True, True, sig=(r == 1))
                m_sc = sch.mark("tensor")
                if expmark[par] is not None:
                    sch.wait("scalar", expmark[par])
                sch.wait("scalar", m_sc)
                sch.emit("scalar", lambda e, par=par, PS=PS, h=h:
                         e.activation(pbuf[par][:, :], PS[:, 0:1024], AF.Exp,
                                      bias=nmub[:, h // 2:h // 2 + 1]), sig=True)
                sch.certify("scalar")
                scmark[par] = sch.mark("scalar")
                if 2 * p_ - 4 * I >= -1:
                    sch.wait("vector", scmark[par])
                    for r in range(2):
                        d_ = 2 * p_ + r - 4 * I
                        if d_ < 0:
                            continue
                        if d_ > 0:
                            sch.emit("vector", lambda e, r=r, par=par, d_=d_:
                                     e.memset(pbuf[par][:, r * 512:r * 512 + 128 * d_], 0.0))
                        sch.emit("vector", lambda e, r=r, par=par, d_=d_:
                                 e.tensor_tensor(
                                     pbuf[par][:, r * 512 + 128 * d_:r * 512 + 128 * (d_ + 1)],
                                     pbuf[par][:, r * 512 + 128 * d_:r * 512 + 128 * (d_ + 1)],
                                     c.triu[:, :], OP.mult), sig=True)
                    sch.certify("vector")
                    pvgate[par] = sch.mark("vector")
                else:
                    pvgate[par] = scmark[par]

            def emit_pv(p_):
                par = p_ % 2
                sch.wait("tensor", pvgate[par])
                if p_ == 0 and pvfree[hpar] is not None:
                    sch.wait("tensor", pvfree[hpar])
                for r in range(2):
                    j = 2 * p_ + r
                    mmi(c, oTb[0:65, 0:512], vv[:, j, h, 0:65],
                        pbuf[par][:, r * 512:(r + 1) * 512],
                        (p_ == 0 and r == 0), (p_ == npairs - 1 and r == 1),
                        sig=True)
                expmark[par] = sch.mark("tensor")

            emit_sc(0)
            for p_ in range(npairs):
                if p_ + 1 < npairs:
                    emit_sc(p_ + 1)
                emit_pv(p_)
            m_pv = sch.mark("tensor")
            sch.wait("scalar", m_pv)
            sch.emit("scalar", lambda e, oTb=oTb: e.copy(rlb[64:65, 0:512],
                                                         oTb[64:65, 0:512]), sig=True)
            sch.certify("scalar")
            sch.wait("tensor", sch.mark("scalar"))
            if bcfree is not None:
                sch.wait("tensor", bcfree)
            mmi(c, c.ps[4][0:64, 0:512], c.onesb[64:65, 0:64], rlb[64:65, 0:512],
                True, True, sig=True)
            sch.wait("vector", sch.mark("tensor"))
            sch.emits("vector", lambda e: e.reciprocal(lb[0:64, 0:512],
                                                       c.ps[4][0:64, 0:512]))
            sch.certify("vector")
            bcfree = sch.mark("vector")
            dsto = omv[pr:pr + 64, m0, I * 512:(I + 1) * 512]
            sch.emit("vector", lambda e, d=dsto, oTb=oTb:
                     e.tensor_tensor(d, oTb[0:64, 0:512], lb[0:64, 0:512], OP.mult),
                     sig=True)
            sch.certify("vector")
            pvfree[hpar] = sch.mark("vector")

    if c.p2_stage < 7:
        pesQ.close()
        c.acc = c.sb("acc", [128, NTI * DM], F32)
        sch.barrier()
        pesA.close()
        return
    pesQ.close()
    sch.wait("sync", sch.mark("vector"))
    sch.wait("sync", sch.mark("scalar"))
    sch.wait("sync", sch.mark("tensor"))
    c.acc = c.sb("acc", [128, NTI * DM], F32)
    c.h3T = c.sb("h3T", [128, DK * S], BF16)
    wo = load_native(c.D["fWo"], DK, DM)
    xnr = [sb(f"p2xnr{i}", [128, DM], F32) for i in range(2)]
    h3b = sb("p2h3b", [128, DM], BF16)
    sqs = sb("p2sqs", [128, DM], BF16)
    ssq = sb("p2ssq", [128, 1], F32)
    ms = sb("p2ms", [128, 1], F32)
    r3_ = sb("p2r3", [128, 1], F32)
    h3v = c.h3T[:, :].rearrange("p (k t) -> p k t", t=S)
    prev2 = [None, None]
    h2mark = [None, None]
    for i in range(NTI):
        par = i % 2
        if prev2[par] is not None:
            sch.wait("sync", prev2[par])
        m_xn = sch.dmas("xnr", lambda s, i=i, par=par:
                        s.dma_start(xnr[par][:, :],
                                    c.D["xn_sp"].ap()[i * 128:(i + 1) * 128, :]))
        if c.h2mode >= 1:
            for sl in range(2):
                pb = c.ps[sl % 2]
                if h2mark[sl] is not None:
                    sch.wait("tensor", h2mark[sl])
                for k in range(DK):
                    mmi(c, pb[:, 0:512], omv[:, k, i * 128:(i + 1) * 128],
                        wo[:, k, sl * 512:(sl + 1) * 512], k == 0, k == DK - 1,
                        sig=(k == DK - 1))
        if c.h2mode >= 2:
            sch.wait("vector", sch.mark("tensor"))
            sch.wait("vector", m_xn)
            for sl in range(2):
                sch.emit("vector", lambda e, sl=sl, i=i, par=par:
                         e.tensor_add(c.acc[:, i * DM + sl * 512:i * DM + (sl + 1) * 512],
                                      xnr[par][:, sl * 512:(sl + 1) * 512],
                                      c.ps[sl][:, 0:512]), sig=True)
                h2mark[sl] = sch.mark("vector")
        else:
            sch.wait("vector", sch.mark("tensor"))
            sch.emit("vector", lambda e: e.nop(nofuse=True), sig=True)
        m_h2 = sch.mark("vector")
        prev2[par] = m_h2
        if c.p2_stage < 9:
            continue
        sch.wait("scalar", m_h2)
        sch.emit("scalar", lambda e, i=i:
                 e.activation(sqs[:, :], c.acc[:, i * DM:(i + 1) * DM], AF.Square,
                              accum_out=ssq[:, :]), sig=True)
        sch.wait("vector", sch.mark("scalar"))
        sch.emits("vector", lambda e: e.tensor_scalar(ms[:, :], ssq[:, :], 1.0 / DM, 1e-6,
                                                      OP.mult, OP.add))
        sch.emits("vector", lambda e: e.reciprocal(ms[:, :], ms[:, :]))
        sch.wait("scalar", sch.mark("vector"))
        sch.emit("scalar", lambda e: e.activation(r3_[:, :], ms[:, :], AF.Sqrt), sig=True)
        sch.wait("vector", sch.mark("scalar"))
        sch.emit("vector", lambda e, i=i:
                 e.scalar_tensor_tensor(h3b[:, :], c.acc[:, i * DM:(i + 1) * DM], r3_[:, :],
                                        c.g2b[:, :], OP.mult, OP.mult), sig=True)
        sch.certify("vector")
        m_h3 = sch.mark("vector")
        sch.wait("tensor", m_h3)
        sch.wait("tensor", sch.mark("scalar"))
        for k in range(DK):
            tri(c, c.pt[:, 128 * k:128 * (k + 1)], h3b[:, 128 * k:128 * (k + 1)],
                sig=(k == DK - 1))
        sch.wait("scalar", sch.mark("tensor"))
        sch.emit("scalar", lambda e, i=i: e.copy(h3v[:, :, i * 128:(i + 1) * 128], ptv),
                 sig=True)
        sch.certify("scalar")
        sch.wait("vector", sch.mark("scalar"))

    sch.barrier()
    pesA.close()


def dump_sbuf(c, buf, dst):
    sch = c.sch
    st = sch.stream("dump", "sync", 2)
    for i in range(NTI):
        sch.dmas(st, lambda s, i=i:
                 s.dma_start(dst.ap()[i * 128:(i + 1) * 128, :], buf[:, i * DM:(i + 1) * DM]))



def p3_moe(c):
    sch = c.sch
    pes = contextlib.ExitStack()
    sb = lambda n, sh, dt, side="left": c.sb(n, sh, dt, side=side, stack=pes)
    gwb = sb("p3gw", [128, DK * NEXP], BF16, side="left")
    comb = sb("p3comb", [128, NTI * NEXP], F32, side="left")
    lgs = sb("p3lgs", [128, NEXP], F32, side="left")
    top8 = sb("p3top8", [128, NEXP], F32, side="left")
    nm1 = sb("p3nm1", [128, 1], F32, side="left")
    nm2 = sb("p3nm2", [128, 1], F32, side="left")
    keep = sb("p3keep", [128, NEXP], F32, side="left")
    pex = sb("p3pex", [128, NEXP], F32, side="left")
    rs = sb("p3rs", [128, 1], F32, side="left")
    w1b = sb("p3w1", [128, DK * EXPD], BF16, side="left")
    w2b = sb("p3w2", [128, 16 * DM], BF16, side="left")
    hid = sb("p3hid", [128, 16 * 512], BF16, side="left")

    h3v = c.h3T[:, :].rearrange("p (k t) -> p k t", t=S)
    gv = gwb[:, :].rearrange("p (k n) -> p k n", n=NEXP)
    w1v = w1b[:, :].rearrange("p (k n) -> p k n", n=EXPD)
    w2v = w2b[:, :].rearrange("p (k n) -> p k n", n=DM)
    hv = hid[:, :].rearrange("p (m n) -> p m n", n=512)
    cbv = comb[:, :].rearrange("p (i n) -> p i n", n=NEXP)

    sch.stream("wl3", "gpsimd", 2)
    m_gw = sch.dmas("wl3", lambda g: g.dma_start(gv, _r3(c.D["gate_w"])))
    sch.wait("tensor", m_gw)

    for i in range(NTI):
        sch.wait("tensor", sch.mark("vector"))
        sch.wait("tensor", sch.mark("scalar"))
        for k in range(DK):
            mmi(c, c.pw[:, 0:NEXP], h3v[:, k, i * 128:(i + 1) * 128], gv[:, k, :],
                k == 0, k == DK - 1, sig=(k == DK - 1))
        sch.wait("vector", sch.mark("tensor"))
        v = "vector"
        sch.emits(v, lambda e: e.tensor_copy(lgs[:, :], c.pw[:, 0:NEXP]))
        sch.emits(v, lambda e: e.max(out=top8[:, :], in_=lgs[:, :]))
        sch.emits(v, lambda e: e.tensor_scalar_mul(nm1[:, :], top8[:, 0:1], -1.0))
        sch.emits(v, lambda e: e.tensor_scalar_mul(nm2[:, :], top8[:, 1:2], -1.0))
        sch.emits(v, lambda e: e.tensor_scalar(keep[:, :], lgs[:, :], nm2[:, :], 0.0,
                                               OP.add, OP.is_ge))
        sch.wait("scalar", sch.mark(v))
        sch.emit("scalar", lambda e: e.activation(pex[:, :], lgs[:, :], AF.Exp,
                                                  bias=nm1[:, :]), sig=True)
        sch.wait(v, sch.mark("scalar"))
        sch.emits(v, lambda e: e.tensor_mul(pex[:, :], pex[:, :], keep[:, :]))
        sch.emits(v, lambda e: e.reduce_sum(rs[:, :], pex[:, :], axis=AX.X))
        sch.emits(v, lambda e: e.reciprocal(rs[:, :], rs[:, :]))
        sch.emit(v, lambda e, i=i: e.tensor_scalar_mul(cbv[:, i, :], pex[:, :], rs[:, :]),
                 sig=True)
        sch.certify(v)

    m_w1 = m_w2 = None
    hmark = [None, None]
    ymark = [None, None]
    for e_ in range(NEXP):
        if e_ == 0:
            m_w1 = sch.dmas("wl3", lambda g, e_=e_: g.dma_start(w1v, _r3(c.D[f"w1_{e_}"])))
            m_w2 = sch.dmas("wl3", lambda g, e_=e_: g.dma_start(w2v, _r3(c.D[f"w2_{e_}"])))
        sch.wait("tensor", m_w1)
        m_hid_last = None
        for sl in range(4):
            for m in range(16):
                par = m % 2
                pb = c.ps[par]
                if hmark[par] is not None:
                    sch.wait("tensor", hmark[par])
                for k in range(DK):
                    mmi(c, pb[:, 0:512], w1v[:, k, m * 128:(m + 1) * 128],
                        h3v[:, k, sl * 512:(sl + 1) * 512], k == 0, k == DK - 1,
                        sig=(k == DK - 1))
                sch.wait("scalar", sch.mark("tensor"))
                sch.emit("scalar", lambda e, m=m, pb=pb:
                         e.activation(hv[:, m, :], pb[:, 0:512], AF.Silu), sig=True)
                hmark[par] = sch.mark("scalar")
            sch.certify("scalar")
            m_hid_last = sch.mark("scalar")
            if sl == 3 and e_ < NEXP - 1:
                sch.wait("gpsimd", sch.mark("tensor"))
                m_w1 = sch.dmas("wl3", lambda g, e_=e_: g.dma_start(w1v, _r3(c.D[f"w1_{e_ + 1}"])))
            sch.wait("tensor", m_w2)
            sch.wait("tensor", m_hid_last)
            for tt in range(4):
                i = sl * 4 + tt
                for dsl in range(2):
                    ypar = (tt * 2 + dsl) % 2
                    pb = c.ps[2 + ypar]
                    if ymark[ypar] is not None:
                        sch.wait("tensor", ymark[ypar])
                    for m in range(16):
                        mmi(c, pb[:, 0:512], hv[:, m, tt * 128:(tt + 1) * 128],
                            w2v[:, m, dsl * 512:(dsl + 1) * 512], m == 0, m == 15,
                            sig=(m == 15))
                    sch.wait("vector", sch.mark("tensor"))
                    sch.emit("vector", lambda e, i=i, dsl=dsl, pb=pb, e_=e_:
                             e.scalar_tensor_tensor(
                                 c.acc[:, i * DM + dsl * 512:i * DM + (dsl + 1) * 512],
                                 pb[:, 0:512], comb[:, i * NEXP + e_:i * NEXP + e_ + 1],
                                 c.acc[:, i * DM + dsl * 512:i * DM + (dsl + 1) * 512],
                                 OP.mult, OP.add), sig=True)
                    ymark[ypar] = sch.mark("vector")
        if e_ < NEXP - 1:
            sch.wait("gpsimd", sch.mark("tensor"))
            m_w2 = sch.dmas("wl3", lambda g, e_=e_: g.dma_start(w2v, _r3(c.D[f"w2_{e_ + 1}"])))
    sch.barrier()
    pes.close()



def p4_final(c):
    sch = c.sch
    sch.stream("x4", "sync", 4)
    sch.stream("s4", "sync", 4)
    sch.stream("y", "sync", 2)
    pes = contextlib.ExitStack()
    sb = lambda n, sh, dt, side="left": c.sb(n, sh, dt, side=side, stack=pes)
    NB4 = 4
    gf = [sb(f"p4g{i}", [128, DM], F32, side="left") for i in range(NB4)]
    xr = [sb(f"p4x{i}", [128, DM], F32, side="left") for i in range(NB4)]
    sr = [sb(f"p4s{i}", [128, DM], F32, side="left") for i in range(NB4)]
    yt = [sb(f"p4y{i}", [128, DM], F32, side="left") for i in range(NB4)]
    prev = [None] * NB4
    prev_y = [None] * NB4
    for i in range(NTI):
        par = i % NB4
        if prev[par] is not None:
            sch.wait("sync", prev[par])
        m_inx = sch.dmas("x4", lambda s, i=i, par=par:
                         s.dma_start(xr[par][:, :], c.D["x"].ap()[i * 128:(i + 1) * 128, :]))
        m_ins = sch.dmas("s4", lambda s, i=i, par=par:
                         s.dma_start(sr[par][:, :],
                                     c.D["s_sp"].ap()[i * 128:(i + 1) * 128, :]))
        if prev[par] is not None:
            sch.wait("scalar", prev[par])
        sch.emit("scalar", lambda e, i=i, par=par:
                 e.activation(gf[par][:, :], c.acc[:, i * DM:(i + 1) * DM], AF.Silu),
                 sig=True)
        sch.certify("scalar")
        sch.wait("vector", sch.mark("scalar"))
        sch.wait("vector", m_inx)
        sch.wait("vector", m_ins)
        if prev_y[par] is not None:
            sch.wait("vector", prev_y[par])
        sch.emit("vector", lambda e, par=par: e.tensor_mul(gf[par][:, :], gf[par][:, :],
                                                           sr[par][:, :]))
        sch.emit("vector", lambda e, par=par: e.tensor_add(yt[par][:, :], xr[par][:, :],
                                                           gf[par][:, :]), sig=True)
        m_y = sch.mark("vector")
        prev[par] = m_y
        sch.wait("sync", m_y)
        prev_y[par] = sch.dmas("y", lambda s, i=i, par=par:
                               s.dma_start(c.D["y"].ap()[i * 128:(i + 1) * 128, :],
                                           yt[par][:, :]))
    if "g" in c.taps:
        pass
    pes.close()


def finish(c):
    sch = c.sch
    if "xn" in c.taps:
        tap_copy_dram(c, c.D["xn_sp"], c.taps["xn"], "txn")
    if "h1" in c.taps:
        tap_copy_dram(c, c.D["h1_sp"], c.taps["h1"], "th1")
    if "s" in c.taps:
        tap_copy_dram(c, c.D["s_sp"], c.taps["s"], "ts")
    if "h2" in c.taps:
        dump_sbuf(c, c.acc, c.taps["h2"])
    if "h4" in c.taps:
        dump_sbuf(c, c.acc, c.taps["h4"])
    for name, (eng, sems, cnts, idx) in c.sch.streams.items():
        for m in sch.stream_done(name):
            sch.wait(eng, m)
    sch.barrier()



_CACHE = {}


def _get_nc(taps=(), upto=4):
    key = (tuple(taps), upto)
    if key not in _CACHE:
        _CACHE[key] = build_program(taps, upto)
    return _CACHE[key]


def make_in_maps(inputs):
    x = np.ascontiguousarray(inputs["L0_attn_hidden_states"], dtype=np.float32)
    norm_w = inputs["L0_attn_norm_w"]
    base = {
        "aWq": inputs["L0_attn_Wq"], "aWk": inputs["L0_attn_Wk"],
        "aWv": inputs["L0_attn_Wv"], "aWo": inputs["L0_attn_Wo"],
        "normw_tok": np.ascontiguousarray(
            np.broadcast_to(np.tile(np.asarray(norm_w, np.float32), AH), (128, DM))),
        "g1b": np.ascontiguousarray(np.broadcast_to(np.asarray(inputs["L0_ffn_g1"], np.float32), (128, DM))),
        "g2b": np.ascontiguousarray(np.broadcast_to(np.asarray(inputs["L0_ffn_g2"], np.float32), (128, DM))),
        "fWq": inputs["L0_ffn_Wq"], "fWdkv": inputs["L0_ffn_Wdkv"],
        "fWuk": inputs["L0_ffn_Wuk"], "fWuv": inputs["L0_ffn_Wuv"],
        "fWo": inputs["L0_ffn_Wo"], "gate_w": inputs["L0_ffn_gate_w"],
        "c_id": np.eye(128, dtype=np.float32),
        "c_triu": np.triu(np.ones((128, 128), dtype=np.float32)),
        "c_ones": np.ones((128, 128), dtype=np.float32),
    }
    for e in range(NEXP):
        base[f"w1_{e}"] = np.ascontiguousarray(inputs["L0_ffn_w1"][e])
        base[f"w2_{e}"] = np.ascontiguousarray(inputs["L0_ffn_w2"][e])
    base = {k: np.ascontiguousarray(np.asarray(v, dtype=np.float32)) for k, v in base.items()}
    maps = []
    for b in range(8):
        m = dict(base)
        m["x"] = np.ascontiguousarray(x[b])
        maps.append(m)
    return maps


def kernel(**inputs):
    nc, c = _get_nc()
    maps = make_in_maps(inputs)
    res = run_bass_kernel_spmd(nc, maps, core_ids=list(range(8)))
    return np.stack([r["y"] for r in res.results], axis=0)


# version 10 (final)
# speedup vs baseline: 1.0842x; 1.0842x over previous
import contextlib
import numpy as np
import ml_dtypes
import concourse.bass as bass
import concourse.mybir as mybir
from concourse.bass_utils import run_bass_kernel_spmd

F32 = mybir.dt.float32
BF16 = mybir.dt.bfloat16
AF = mybir.ActivationFunctionType
OP = mybir.AluOpType
AX = mybir.AxisListType
BF = ml_dtypes.bfloat16

S = 2048
DM = 1024
NTI = 16
DK = 8
AH = 8
MH = 16
LAT = 512
NEXP = 8
EXPD = 2048

ENGS = ("sync", "gpsimd", "tensor", "vector", "scalar")


class Sched:
    def __init__(self, nc, es):
        self.nc = nc
        self.es = es
        self.q = {e: [] for e in ENGS}
        self.sem = {e: es.enter_context(nc.semaphore(f"sem_{e}")) for e in ENGS}
        self.cnt = {e: 0 for e in ENGS}
        self.streams = {}

    def stream(self, name, eng, ns=2):
        if name not in self.streams:
            sems = [self.es.enter_context(self.nc.semaphore(f"dma_{name}{i}"))
                    for i in range(ns)]
            self.streams[name] = [eng, sems, [0] * ns, 0]
        return name

    def dmas(self, name, build):
        eng, sems, cnts, idx = self.streams[name]
        s = sems[idx % len(sems)]
        i = idx % len(sems)
        if cnts[i] > 0:
            self.q[eng].append(lambda e, s=s, v=16 * cnts[i]: e.wait_ge(s, v))
        cnts[i] += 1
        self.streams[name][3] = idx + 1

        def fn(e, b=build, s=s):
            b(e).then_inc(s, 16)
        self.q[eng].append(fn)
        return (s, 16 * cnts[i])

    def quiesce(self, names, engines=ENGS):
        for n in names:
            if n in self.streams:
                for m in self.stream_done(n):
                    for e in engines:
                        self.wait(e, m)

    def stream_done(self, name):
        eng, sems, cnts, idx = self.streams[name]
        return [(sems[i], 16 * cnts[i]) for i in range(len(sems)) if cnts[i] > 0]

    def emit(self, eng, build, sig=False):
        if eng in ("vector", "scalar"):
            if self.cnt[eng] > 0:
                self.q[eng].append(
                    lambda e, s=self.sem[eng], v=self.cnt[eng]: e.wait_ge(s, v))
            sig = True
        if sig:
            self.cnt[eng] += 1
            sem = self.sem[eng]

            def fn(e, b=build, sem=sem):
                b(e).then_inc(sem, 1)
        else:
            def fn(e, b=build):
                b(e)
        self.q[eng].append(fn)

    def dma(self, eng, build):
        self.cnt[eng] += 16
        sem = self.sem[eng]

        def fn(e, b=build, sem=sem):
            b(e).then_inc(sem, 16)
        self.q[eng].append(fn)

    def certify(self, eng):
        self.wait(eng, self.mark(eng))
        self.emit(eng, lambda e: e.nop(nofuse=True), sig=True)

    def emits(self, eng, build):
        self.emit(eng, build, sig=True)
        self.wait(eng, self.mark(eng))

    def mark(self, eng):
        return (self.sem[eng], self.cnt[eng])

    def wait(self, eng, mark):
        sem, val = mark
        if val > 0:
            self.q[eng].append(lambda e, sem=sem, val=val: e.wait_ge(sem, val))

    def wait_all(self, eng, *marks):
        for m in marks:
            self.wait(eng, m)

    def barrier(self):
        for name, (eng, sems, cnts, idx) in self.streams.items():
            for m in self.stream_done(name):
                self.wait(eng, m)
        marks = {}
        for e in ENGS:
            self.wait(e, self.mark(e))
            self.emit(e, lambda en: en.nop(nofuse=True), sig=True)
            marks[e] = self.mark(e)
        for e in ENGS:
            for e2 in ENGS:
                if e2 != e:
                    self.wait(e, marks[e2])

    def replay(self, block):
        for name in ENGS:
            ops = self.q[name]
            getattr(block, name)(lambda engine, ops=ops: [fn(engine) for fn in ops])


def _r3(dram, p=128):
    return dram.ap().rearrange("(k p) n -> p k n", p=p)


class Ctx:
    pass


class PsHalf:

    def __init__(self, t, off):
        self.t, self.off = t, off

    def __getitem__(self, idx):
        p, f = idx
        lo = (f.start or 0) + self.off
        hi = (512 if f.stop is None else f.stop) + self.off
        return self.t[p, lo:hi]


def mmi(c, out_ap, lhsT_ap, rhs_ap, start, stop, sig=False):
    c.sch.emit("tensor",
               lambda t, o=out_ap, l=lhsT_ap, r=rhs_ap, st=start, sp=stop:
               t.matmul(o, l, r, start=st, stop=sp), sig=sig)


def tri(c, out_ap, in_ap, sig=False):
    c.sch.emit("tensor",
               lambda t, o=out_ap, i=in_ap: t.transpose(o, i, c.ident[:, :]), sig=sig)



def build_program(taps=(), upto=4, dbg=False, p2_heads=MH, p2_mub=True, p2_stage=9, h2mode=3):
    nc = bass.Bass("TRN2", target_bir_lowering=False)
    c = Ctx()
    c.nc = nc
    es = contextlib.ExitStack()
    c.es = es

    D = {}
    D["x"] = nc.dram_tensor("x", [S, DM], F32, kind="ExternalInput")
    for n, sh in [("aWq", [DM, DM]), ("aWk", [DM, DM]), ("aWv", [DM, DM]), ("aWo", [DM, DM]),
                  ("normw_tok", [128, DM]), ("g1b", [128, DM]), ("g2b", [128, DM]),
                  ("fWq", [DM, DM]), ("fWdkv", [DM, LAT]), ("fWuk", [LAT, DM]),
                  ("fWuv", [LAT, DM]), ("fWo", [DM, DM]), ("gate_w", [DM, NEXP]),
                  ("c_id", [128, 128]), ("c_triu", [128, 128]), ("c_ones", [128, 128])]:
        D[n] = nc.dram_tensor(n, sh, F32, kind="ExternalInput")
    for e in range(NEXP):
        D[f"w1_{e}"] = nc.dram_tensor(f"w1_{e}", [DM, EXPD], F32, kind="ExternalInput")
        D[f"w2_{e}"] = nc.dram_tensor(f"w2_{e}", [EXPD, DM], F32, kind="ExternalInput")
    D["xn_sp"] = nc.dram_tensor("xn_sp", [S, DM], F32, kind="Internal")
    D["s_sp"] = nc.dram_tensor("s_sp", [S, DM], F32, kind="Internal")
    D["h1_sp"] = nc.dram_tensor("h1_sp", [S, DM], F32, kind="Internal")
    D["y"] = nc.dram_tensor("y", [S, DM], F32, kind="ExternalOutput")
    c.taps = {}
    for t in taps:
        c.taps[t] = nc.dram_tensor("tap_" + t, [S, DM], F32, kind="ExternalOutput")
    c.p2_heads = p2_heads
    c.p2_mub = p2_mub
    c.p2_stage = p2_stage
    c.h2mode = h2mode
    c.dbg = {}
    if dbg:
        for n, sh in [("qT", [128, AH * S]), ("kT", [128, AH * S]),
                      ("vt", [S, DM]), ("kt", [S, DM]), ("o2", [S, DM]),
                      ("qz", [S, AH]), ("scm", [S, AH * 128]), ("st", [S, AH * 128]),
                      ("car", [S, AH]), ("sout", [S, DM])]:
            c.dbg[n] = nc.dram_tensor("dbg_" + n, sh, F32, kind="ExternalOutput")
    c.D = D

    with es:
        sch = Sched(nc, es)
        c.sch = sch

        def sb(name, shape, dt, side="right", stack=None):
            return (stack or es).enter_context(nc.sbuf_tensor(name, shape, dt, side=side))
        c.sb = sb

        c.ident = sb("ident", [128, 128], BF16)
        c.triu = sb("triu", [128, 128], BF16)
        c.onesb = sb("onesb", [128, 128], BF16)
        c.normw = sb("normw", [128, DM], F32)
        c.g1b = sb("g1b_s", [128, DM], F32)
        c.g2b = sb("g2b_s", [128, DM], F32)

        c.pA = es.enter_context(nc.psum_tensor("pA", [128, 1024], F32))
        c.pB = es.enter_context(nc.psum_tensor("pB", [128, 1024], F32))
        ps4 = es.enter_context(nc.psum_tensor("ps4", [128, 512], F32))
        ps5 = es.enter_context(nc.psum_tensor("ps5", [128, 512], F32))
        c.ps = [PsHalf(c.pA, 0), PsHalf(c.pA, 512), PsHalf(c.pB, 0), PsHalf(c.pB, 512),
                ps4, ps5]
        c.pw = es.enter_context(nc.psum_tensor("pw", [128, 512], F32))
        c.pt = es.enter_context(nc.psum_tensor("pt", [128, 1024], BF16))

        sch.stream("cg", "gpsimd", 1)
        sch.stream("cs", "sync", 1)
        for dst, src in [(c.ident, D["c_id"]), (c.triu, D["c_triu"]), (c.onesb, D["c_ones"])]:
            c.m_const_g = sch.dmas("cg", lambda g, d=dst, s=src: g.dma_start(d[:, :], s.ap()))
        for dst, src in [(c.normw, D["normw_tok"]), (c.g1b, D["g1b"]), (c.g2b, D["g2b"])]:
            c.m_const_s = sch.dmas("cs", lambda s, d=dst, r=src: s.dma_start(d[:, :], r.ap()))
        for e in ("tensor", "vector", "scalar"):
            sch.wait(e, c.m_const_g)
            sch.wait(e, c.m_const_s)

        c.es01 = contextlib.ExitStack()
        p0_ln(c)
        if upto >= 1:
            p1_attn(c)
        c.es01.close()
        if upto >= 2:
            p2_mla(c)
        if upto >= 3:
            p3_moe(c)
        if upto >= 4:
            p4_final(c)
        finish(c)

        with nc.Block() as block:
            sch.replay(block)
    return nc, c



def p0_ln(c):
    sch, sb = c.sch, c.sb
    pes = contextlib.ExitStack()
    sb = lambda n, sh, dt, side="left": c.sb(n, sh, dt, side=side, stack=pes)
    c.xnT = c.sb("xnT", [128, DK * S], BF16, side="left", stack=c.es01)
    xf = [sb(f"p0x{i}", [128, DM], F32, side="left") for i in range(2)]
    xnf = [sb(f"p0xn{i}", [128, DM], F32, side="left") for i in range(2)]
    xnb = [sb(f"p0xnb{i}", [128, DM], BF16, side="left") for i in range(2)]
    h1b = [sb(f"p0h1b{i}", [128, DM], BF16, side="left") for i in range(2)]
    sqs = sb("p0sqs", [128, DM], BF16, side="left")
    s1 = sb("p0s1", [128, 1], F32, side="left")
    mm2 = sb("p0mm2", [128, 1], F32, side="left")
    nm = sb("p0nm", [128, 1], F32, side="left")
    iv = sb("p0iv", [128, 1], F32, side="left")
    rstd = sb("p0rstd", [128, 1], F32, side="left")
    ssq = sb("p0ssq", [128, 1], F32, side="left")
    ms = sb("p0ms", [128, 1], F32, side="left")
    r2 = sb("p0r2", [128, 1], F32, side="left")

    xnTv = c.xnT[:, :].rearrange("p (k t) -> p k t", t=S)
    ptv = c.pt[:, :].rearrange("p (k t) -> p k t", t=128)

    sch.stream("x0", "sync", 3)
    sch.stream("xnsp", "sync", 2)
    sch.stream("h1sp", "gpsimd", 2)
    free_m = [None, None]
    evac_m = [None, None]
    h1sp_m = [None, None]
    spill_m = [None, None]
    for i in range(NTI):
        par = i % 2
        if free_m[par] is not None:
            sch.wait("sync", free_m[par])
        if h1sp_m[par] is not None:
            sch.wait("vector", h1sp_m[par])
        if spill_m[par] is not None:
            sch.wait("vector", spill_m[par])
        m_in = sch.dmas("x0", lambda s, d=xf[par], i=i:
                        s.dma_start(d[:, :], c.D["x"].ap()[i * 128:(i + 1) * 128, :]))

        sch.wait("vector", m_in)
        sch.wait("scalar", m_in)
        sch.emits("vector", lambda e, x=xf[par]: e.reduce_sum(s1[:, :], x[:, :], axis=AX.X))
        sch.emits("vector", lambda e: e.tensor_scalar_mul(nm[:, :], s1[:, :], -1.0 / DM))
        sch.emit("scalar", lambda e, x=xf[par]:
                 e.activation(sqs[:, :], x[:, :], AF.Square, accum_out=ssq[:, :]), sig=True)
        sch.wait("vector", sch.mark("scalar"))
        sch.emits("vector", lambda e: e.tensor_mul(mm2[:, :], nm[:, :], nm[:, :]))
        sch.emits("vector", lambda e: e.tensor_scalar(iv[:, :], ssq[:, :], 1.0 / DM, 1e-5,
                                                      OP.mult, OP.add))
        sch.emits("vector", lambda e: e.tensor_sub(iv[:, :], iv[:, :], mm2[:, :]))
        sch.emits("vector", lambda e: e.reciprocal(iv[:, :], iv[:, :]))
        sch.wait("scalar", sch.mark("vector"))
        sch.emit("scalar", lambda e: e.activation(rstd[:, :], iv[:, :], AF.Sqrt), sig=True)
        sch.wait("vector", sch.mark("scalar"))
        sch.emit("vector", lambda e, x=xf[par], xn=xnf[par]:
                 e.tensor_scalar(xn[:, :], x[:, :], nm[:, :], rstd[:, :], OP.add, OP.mult))
        sch.emit("vector", lambda e, xn=xnf[par], xb=xnb[par]:
                 e.tensor_copy(xb[:, :], xn[:, :]), sig=True)
        m_xn = sch.mark("vector")

        sch.wait("sync", m_xn)
        m_xnsp = sch.dmas("xnsp", lambda s, xn=xnf[par], i=i:
                          s.dma_start(c.D["xn_sp"].ap()[i * 128:(i + 1) * 128, :], xn[:, :]))
        spill_m[par] = m_xnsp

        sch.wait("scalar", m_xn)
        sch.emit("scalar", lambda e, xn=xnf[par]:
                 e.activation(sqs[:, :], xn[:, :], AF.Square, accum_out=ssq[:, :]), sig=True)
        sch.wait("vector", sch.mark("scalar"))
        sch.emits("vector", lambda e: e.tensor_scalar(ms[:, :], ssq[:, :], 1.0 / DM, 1e-6,
                                                      OP.mult, OP.add))
        sch.emits("vector", lambda e: e.reciprocal(ms[:, :], ms[:, :]))
        sch.wait("scalar", sch.mark("vector"))
        sch.emit("scalar", lambda e: e.activation(r2[:, :], ms[:, :], AF.Sqrt), sig=True)
        sch.wait("vector", sch.mark("scalar"))
        sch.emit("vector", lambda e, xn=xnf[par], hb=h1b[par]:
                 e.scalar_tensor_tensor(hb[:, :], xn[:, :], r2[:, :], c.g1b[:, :],
                                        OP.mult, OP.mult), sig=True)
        m_h1 = sch.mark("vector")
        free_m[par] = m_h1

        sch.wait("gpsimd", m_h1)
        h1sp_m[par] = sch.dmas("h1sp", lambda g, hb=h1b[par], i=i:
                               g.dma_start(c.D["h1_sp"].ap()[i * 128:(i + 1) * 128, :],
                                           hb[:, :]))

        sch.wait("tensor", m_xn)
        if evac_m[par] is not None:
            sch.wait("tensor", evac_m[par])
        for k in range(DK):
            tri(c, c.pt[:, 128 * k:128 * (k + 1)], xnb[par][:, 128 * k:128 * (k + 1)],
                sig=(k == DK - 1))
        sch.wait("scalar", sch.mark("tensor"))
        sch.emit("scalar", lambda e, i=i: e.copy(xnTv[:, :, i * 128:(i + 1) * 128], ptv),
                 sig=True)
        sch.certify("scalar")
        evac_m[par] = sch.mark("scalar")
    sch.barrier()
    pes.close()



def p1_attn(c):
    sch = c.sch
    sch.stream("wl", "gpsimd", 2)
    sch.stream("ssp", "sync", 2)
    pes = contextlib.ExitStack()
    sb = lambda n, sh, dt, side="left": c.sb(n, sh, dt, side=side, stack=pes)
    wnat = sb("p1wnat", [128, DK * DM], BF16, side="left")
    wqT = sb("p1wqT", [128, DK * DM], BF16, side="left")
    qT = sb("p1qT", [128, AH * S], BF16, side="left")
    kT = sb("p1kT", [128, AH * S], BF16, side="left")
    wvT = sb("p1wvT", [128, DK * DM], BF16, side="left")
    woT = sb("p1woT", [128, DK * DM], BF16, side="left")

    ptv = c.pt[:, :].rearrange("p (k t) -> p k t", t=128)
    xv = c.xnT[:, :].rearrange("p (k t) -> p k t", t=S)

    def load_and_transpose(dstT, dram):
        sch.wait("gpsimd", sch.mark("tensor"))
        m_ld = sch.dmas("wl", lambda g: g.dma_start(
            wnat[:, :].rearrange("p (k n) -> p k n", n=DM), _r3(dram)))
        sch.wait("tensor", m_ld)
        dv = dstT[:, :].rearrange("p (k n) -> p k n", n=DM)
        for me in range(DK):
            sch.wait("tensor", sch.mark("scalar"))
            for kd in range(DK):
                tri(c, c.pt[:, 128 * kd:128 * (kd + 1)],
                    wnat[:, me * DM + 128 * kd:me * DM + 128 * (kd + 1)], sig=(kd == DK - 1))
            sch.wait("scalar", sch.mark("tensor"))
            sch.emit("scalar", lambda e, d=dv[:, :, me * 128:(me + 1) * 128]:
                     e.copy(d, ptv), sig=True)
        sch.certify("scalar")
        sch.wait("tensor", sch.mark("scalar"))

    def projB(dstv, wv, kk, n_mt):
        bmark = [None, None]
        for m0 in range(n_mt):
            for sl in range(4):
                par = sl % 2
                pb = c.ps[par]
                if bmark[par] is not None:
                    sch.wait("tensor", bmark[par])
                for k in range(kk):
                    mmi(c, pb[:, 0:512], wv[:, k, m0 * 128:(m0 + 1) * 128],
                        xv[:, k, sl * 512:(sl + 1) * 512], k == 0, k == kk - 1,
                        sig=(k == kk - 1))
                sch.wait("scalar", sch.mark("tensor"))
                sch.emit("scalar", lambda e, d=dstv[:, m0, sl * 512:(sl + 1) * 512], p=pb:
                         e.copy(d, p[:, 0:512]), sig=True)
                bmark[par] = sch.mark("scalar")
        sch.certify("scalar")
        sch.wait("tensor", sch.mark("scalar"))

    load_and_transpose(wqT, c.D["aWq"])
    qTv = qT[:, :].rearrange("p (h t) -> p h t", t=S)
    projB(qTv, wqT[:, :].rearrange("p (k n) -> p k n", n=DM), DK, AH)
    load_and_transpose(wqT, c.D["aWk"])
    kTv = kT[:, :].rearrange("p (h t) -> p h t", t=S)
    projB(kTv, wqT[:, :].rearrange("p (k n) -> p k n", n=DM), DK, AH)
    if c.dbg:
        sch.stream("dbg", "gpsimd", 1)
        sch.wait("gpsimd", sch.mark("scalar"))
        sch.dmas("dbg", lambda g: g.dma_start(c.dbg["qT"].ap(), qT[:, :]))
        sch.dmas("dbg", lambda g: g.dma_start(c.dbg["kT"].ap(), kT[:, :]))
    load_and_transpose(wvT, c.D["aWv"])
    load_and_transpose(woT, c.D["aWo"])
    wvv = wvT[:, :].rearrange("p (k n) -> p k n", n=DM)
    wov = woT[:, :].rearrange("p (k n) -> p k n", n=DM)

    state_f = sb("p1state", [128, AH * 128], F32, side="left")
    state_b = sb("p1stateb", [128, AH * 128], BF16, side="left")
    carry_f = sb("p1carry", [128, AH], F32, side="left")
    carry_b = sb("p1carryb", [128, AH], BF16, side="left")
    dcar = sb("p1dcar", [128, AH], F32, side="left")
    vtok = [sb(f"p1vt{i}", [128, DM], BF16, side="left") for i in range(2)]
    ktok = [sb(f"p1kt{i}", [128, DM], BF16, side="left") for i in range(2)]
    scm = [sb(f"p1scm{i}", [128, AH * 128], BF16, side="left") for i in range(2)]
    o2b = sb("p1o2b", [128, DM], BF16, side="left")
    oTc = sb("p1oTc", [128, DM], BF16, side="left")
    o1f = sb("p1o1f", [128, DM], F32, side="left")
    sqf = sb("p1sqf", [128, DM], F32, side="left")
    qzf = sb("p1qz", [128, AH], F32, side="left")
    rdn = sb("p1rdn", [128, AH], F32, side="left")
    ssq8 = sb("p1ssq8", [128, AH], F32, side="left")
    rst8 = sb("p1rst8", [128, AH], F32, side="left")
    sout = sb("p1sout", [128, DM], F32, side="left")

    for t_ in (state_f, state_b, carry_f, carry_b):
        sch.emit("vector", lambda e, t=t_: e.memset(t[:, :], 0.0), sig=(t_ is carry_b))
    sch.certify("vector")
    sch.wait("tensor", sch.mark("vector"))
    m_o1read = m_qzread = m_ots_p = m_state_p = m_sout_p = m_ot_p = m_sp_p = m_ssp_p = None
    m_stb = None

    for ci in range(NTI):
        par = ci % 2
        t0, t1 = ci * 128, (ci + 1) * 128

        if m_o1read is not None:
            sch.wait("tensor", m_o1read)
        for sl in range(2):
            for k in range(DK):
                mmi(c, c.ps[sl][:, 0:512], xv[:, k, t0:t1], wvv[:, k, sl * 512:(sl + 1) * 512],
                    k == 0, k == DK - 1, sig=(k == DK - 1))
        sch.wait("scalar", sch.mark("tensor"))
        for sl in range(2):
            sch.emit("scalar", lambda e, sl=sl, d=vtok[par]:
                     e.copy(d[:, sl * 512:(sl + 1) * 512], c.ps[sl][:, 0:512]), sig=(sl == 1))
        sch.certify("scalar")
        m_vs = sch.mark("scalar")

        if m_ots_p is not None:
            sch.wait("tensor", m_ots_p)
        for h in range(AH):
            tri(c, c.pt[:, 128 * h:128 * (h + 1)], kTv[:, h, t0:t1], sig=(h == AH - 1))
        sch.wait("scalar", sch.mark("tensor"))
        sch.emit("scalar", lambda e, d=ktok[par]: e.copy(d[:, :], c.pt[:, :]), sig=True)
        sch.certify("scalar")
        m_ks = sch.mark("scalar")
        if c.dbg:
            sch.wait("gpsimd", m_vs)
            sch.dmas("dbg", lambda g, ci=ci, d=vtok[par]:
                     g.dma_start(c.dbg["vt"].ap()[ci * 128:(ci + 1) * 128, :], d[:, :]))
            sch.wait("gpsimd", m_ks)
            sch.dmas("dbg", lambda g, ci=ci, d=ktok[par]:
                     g.dma_start(c.dbg["kt"].ap()[ci * 128:(ci + 1) * 128, :], d[:, :]))

        if m_sout_p is not None:
            sch.wait("tensor", m_sout_p)
        for h in range(AH):
            pb = c.ps[2 + h // 4]
            mmi(c, pb[:, 128 * (h % 4):128 * (h % 4 + 1)], kTv[:, h, t0:t1], qTv[:, h, t0:t1],
                True, True, sig=(h == AH - 1))
        sch.wait("vector", sch.mark("tensor"))
        tri_b = bass.AP(c.triu.handle if hasattr(c.triu, "handle") else c.triu, 0,
                        [[128, 128], [0, 4], [1, 128]])
        for half in range(2):
            sch.emit("vector", lambda e, half=half, d=scm[par], tb=tri_b:
                     e.tensor_tensor(
                         d[:, half * 512:(half + 1) * 512].rearrange("p (h t) -> p h t", t=128),
                         c.ps[2 + half][:, 0:512].rearrange("p (h t) -> p h t", t=128),
                         tb, OP.mult), sig=(half == 1))
        sch.certify("vector")
        m_scm = sch.mark("vector")
        if c.dbg:
            sch.wait("gpsimd", m_scm)
            sch.dmas("dbg", lambda g, ci=ci, d=scm[par]:
                     g.dma_start(c.dbg["scm"].ap()[ci * 128:(ci + 1) * 128, :], d[:, :]))
            sch.dmas("dbg", lambda g, ci=ci:
                     g.dma_start(c.dbg["car"].ap()[ci * 128:(ci + 1) * 128, 0:AH],
                                 carry_b[:, :]))

        sch.wait("tensor", m_scm)
        sch.wait("tensor", m_vs)
        sch.wait("tensor", m_ks)
        if m_qzread is not None:
            sch.wait("tensor", m_qzread)
        if m_state_p is not None:
            sch.wait("tensor", m_state_p)
        if m_stb is not None:
            sch.wait("tensor", m_stb)
        for h in range(AH):
            ob = c.ps[h // 4][:, 128 * (h % 4):128 * (h % 4 + 1)]
            qb = c.pw[:, h:h + 1]
            if ci > 0:
                mmi(c, ob, qTv[:, h, t0:t1], state_b[:, 128 * h:128 * (h + 1)], True, False)
                mmi(c, qb, qTv[:, h, t0:t1], carry_b[:, h:h + 1], True, False)
                mmi(c, ob, scm[par][:, 128 * h:128 * (h + 1)],
                    vtok[par][:, 128 * h:128 * (h + 1)], False, True)
                mmi(c, qb, scm[par][:, 128 * h:128 * (h + 1)], c.onesb[:, 0:1], False, True,
                    sig=(h == AH - 1))
            else:
                mmi(c, ob, scm[par][:, 128 * h:128 * (h + 1)],
                    vtok[par][:, 128 * h:128 * (h + 1)], True, True)
                mmi(c, qb, scm[par][:, 128 * h:128 * (h + 1)], c.onesb[:, 0:1], True, True,
                    sig=(h == AH - 1))
        m_o = sch.mark("tensor")

        if m_state_p is not None:
            sch.wait("tensor", m_state_p)
        for h in range(AH):
            pb = c.ps[4 + h // 4]
            mmi(c, pb[:, 128 * (h % 4):128 * (h % 4 + 1)], ktok[par][:, 128 * h:128 * (h + 1)],
                vtok[par][:, 128 * h:128 * (h + 1)], True, True, sig=(h == AH - 1))
        m_sd = sch.mark("tensor")
        if ci < NTI - 1:
            sch.wait("vector", m_sd)
            for half in range(2):
                sch.emit("vector", lambda e, half=half:
                         e.tensor_add(state_f[:, half * 512:(half + 1) * 512],
                                      state_f[:, half * 512:(half + 1) * 512],
                                      c.ps[4 + half][:, 0:512]))
            sch.emit("vector", lambda e: e.nop(nofuse=True), sig=True)
            sch.wait("scalar", sch.mark("vector"))
            sch.emit("scalar", lambda e: e.copy(state_b[:, :], state_f[:, :]), sig=True)
            sch.certify("scalar")
            m_stb = sch.mark("scalar")
            sch.emits("vector", lambda e, src=kTv[:, :, t0:t1]:
                      e.reduce_sum(dcar[:, :], src, axis=AX.X))
            sch.emits("vector", lambda e: e.tensor_add(carry_f[:, :], carry_f[:, :], dcar[:, :]))
            sch.emit("vector", lambda e: e.tensor_copy(carry_b[:, :], carry_f[:, :]), sig=True)
            sch.certify("vector")
        m_state = sch.mark("vector")
        m_state_p = m_state

        sch.wait("vector", m_o)
        sch.emits("vector", lambda e: e.tensor_copy(qzf[:, :], c.pw[:, 0:AH]))
        m_qzread = sch.mark("vector")
        sch.emits("vector", lambda e: e.scalar_tensor_tensor(rdn[:, :], qzf[:, :], -1.0,
                                                             qzf[:, :], OP.mult, OP.max))
        sch.emits("vector", lambda e: e.tensor_scalar_max(rdn[:, :], rdn[:, :], 1.0))
        sch.emits("vector", lambda e: e.reciprocal(rdn[:, :], rdn[:, :]))
        for half in range(2):
            sch.emit("vector", lambda e, half=half:
                     e.tensor_tensor(
                         o1f[:, half * 512:(half + 1) * 512].rearrange("p (h t) -> p h t", t=128),
                         c.ps[half][:, 0:512].rearrange("p (h t) -> p h t", t=128),
                         rdn[:, half * 4:half * 4 + 4].to_broadcast([128, 4, 128]),
                         OP.mult), sig=(half == 1))
        m_o1read = sch.mark("vector")
        sch.wait("scalar", sch.mark("vector"))
        sch.emit("scalar", lambda e: e.activation(sqf[:, :], o1f[:, :], AF.Square), sig=True)
        sch.wait("vector", sch.mark("scalar"))
        sch.emits("vector", lambda e: e.reduce_sum(
            ssq8[:, :], sqf[:, :].rearrange("p (h t) -> p h t", t=128), axis=AX.X))
        sch.emits("vector", lambda e: e.tensor_scalar(ssq8[:, :], ssq8[:, :], 1.0 / 128, 1e-5,
                                                      OP.mult, OP.add))
        sch.emits("vector", lambda e: e.reciprocal(ssq8[:, :], ssq8[:, :]))
        sch.wait("scalar", sch.mark("vector"))
        sch.emit("scalar", lambda e: e.activation(rst8[:, :], ssq8[:, :], AF.Sqrt), sig=True)
        sch.wait("vector", sch.mark("scalar"))
        sch.emit("vector", lambda e:
                 e.tensor_tensor(o1f[:, :].rearrange("p (h t) -> p h t", t=128),
                                 o1f[:, :].rearrange("p (h t) -> p h t", t=128),
                                 rst8[:, 0:AH].to_broadcast([128, AH, 128]), OP.mult))
        if m_ot_p is not None:
            sch.wait("vector", m_ot_p)
        sch.emit("vector", lambda e: e.tensor_tensor(o2b[:, :], o1f[:, :], c.normw[:, :],
                                                     OP.mult), sig=True)
        sch.certify("vector")
        m_o2 = sch.mark("vector")
        if c.dbg:
            sch.wait("gpsimd", m_o2)
            sch.dmas("dbg", lambda g, ci=ci:
                     g.dma_start(c.dbg["qz"].ap()[ci * 128:(ci + 1) * 128, :], qzf[:, :]))
            sch.dmas("dbg", lambda g, ci=ci:
                     g.dma_start(c.dbg["o2"].ap()[ci * 128:(ci + 1) * 128, :], o2b[:, :]))
            m_dbg = sch.dmas("dbg", lambda g, ci=ci:
                             g.dma_start(c.dbg["st"].ap()[ci * 128:(ci + 1) * 128, :],
                                         state_b[:, :]))
            sch.wait("vector", m_dbg)
            sch.wait("tensor", m_dbg)

        sch.wait("tensor", m_o2)
        for k in range(DK):
            tri(c, c.pt[:, 128 * k:128 * (k + 1)], o2b[:, 128 * k:128 * (k + 1)],
                sig=(k == DK - 1))
        m_ot_p = sch.mark("tensor")
        if m_sp_p is not None:
            sch.wait("scalar", m_sp_p)
        sch.wait("scalar", sch.mark("tensor"))
        sch.emit("scalar", lambda e: e.copy(oTc[:, :], c.pt[:, :]), sig=True)
        sch.certify("scalar")
        m_ots_p = sch.mark("scalar")
        sch.wait("tensor", m_ots_p)
        for sl in range(2):
            pb = c.ps[2 + sl]
            for k in range(DK):
                mmi(c, pb[:, 0:512], oTc[:, 128 * k:128 * (k + 1)],
                    wov[:, k, sl * 512:(sl + 1) * 512], k == 0, k == DK - 1,
                    sig=(k == DK - 1))
        m_sp_p = sch.mark("tensor")
        sch.wait("scalar", m_sp_p)
        if m_ssp_p is not None:
            sch.wait("scalar", m_ssp_p)
        for sl in range(2):
            sch.emit("scalar", lambda e, sl=sl:
                     e.copy(sout[:, sl * 512:(sl + 1) * 512], c.ps[2 + sl][:, 0:512]),
                     sig=(sl == 1))
        sch.certify("scalar")
        m_sout = sch.mark("scalar")
        m_sout_p = m_sout
        sch.wait("sync", m_sout)
        m_ssp = sch.dmas("ssp", lambda s, ci=ci:
                         s.dma_start(c.D["s_sp"].ap()[ci * 128:(ci + 1) * 128, :], sout[:, :]))
        m_ssp_p = m_ssp
        if c.dbg:
            sch.wait("gpsimd", m_sout)
            m_dbg2 = sch.dmas("dbg", lambda g, ci=ci:
                              g.dma_start(c.dbg["sout"].ap()[ci * 128:(ci + 1) * 128, :],
                                          sout[:, :]))
            sch.wait("vector", m_dbg2)

    sch.barrier()
    pes.close()


def tap_copy_dram(c, src, dst, nm):
    sch = c.sch
    pes = contextlib.ExitStack()
    buf = c.sb("tap_" + nm, [128, DM], F32, side="left", stack=pes)
    st = sch.stream("tap_" + nm, "sync", 1)
    for i in range(NTI):
        sch.dmas(st, lambda s, i=i: s.dma_start(buf[:, :], src.ap()[i * 128:(i + 1) * 128, :]))
        sch.dmas(st, lambda s, i=i: s.dma_start(dst.ap()[i * 128:(i + 1) * 128, :], buf[:, :]))
    pes.close()



def p2_mla(c):
    sch = c.sch
    pesA = contextlib.ExitStack()
    pesQ = contextlib.ExitStack()
    pesH = contextlib.ExitStack()
    sb = lambda n, sh, dt, side="left", stack=None: c.sb(n, sh, dt, side=side, stack=stack or pesA)

    wbuf = sb("p2w", [128, DK * DM], BF16)
    omla = sb("p2omla", [128, DK * S], BF16)
    omv = omla[:, :].rearrange("p (m t) -> p m t", t=S)
    qT = sb("p2qT", [128, DK * S], BF16, stack=pesQ)
    cT = sb("p2cT", [128, 4 * S], BF16, stack=pesQ)
    h1T = sb("p2h1T", [128, DK * S], BF16, stack=pesH)

    ptv = c.pt[:, :].rearrange("p (k t) -> p k t", t=128)
    h1v = h1T[:, :].rearrange("p (k t) -> p k t", t=S)
    qv = qT[:, :].rearrange("p (m t) -> p m t", t=S)
    cv = cT[:, :].rearrange("p (m t) -> p m t", t=S)

    sch.stream("h1ld", "gpsimd", 2)
    sch.stream("wl2", "gpsimd", 2)
    sch.stream("xnr", "sync", 2)
    h1tok = [sb(f"p2h1t{i}", [128, DM], BF16, stack=pesH) for i in range(2)]
    prev = [None, None]
    for i in range(NTI):
        par = i % 2
        if prev[par] is not None:
            sch.wait("gpsimd", prev[par])
        m_h1ld = sch.dmas("h1ld", lambda g, i=i, par=par:
                          g.dma_start(h1tok[par][:, :],
                                      c.D["h1_sp"].ap()[i * 128:(i + 1) * 128, :]))
        sch.wait("tensor", m_h1ld)
        sch.wait("tensor", sch.mark("scalar"))
        for k in range(DK):
            tri(c, c.pt[:, 128 * k:128 * (k + 1)], h1tok[par][:, 128 * k:128 * (k + 1)],
                sig=(k == DK - 1))
        sch.wait("scalar", sch.mark("tensor"))
        sch.emit("scalar", lambda e, i=i:
                 e.copy(h1v[:, :, i * 128:(i + 1) * 128], ptv), sig=True)
        sch.certify("scalar")
        prev[par] = sch.mark("scalar")
    sch.wait("tensor", sch.mark("scalar"))

    def load_native(dram, ktiles, ncols):
        sch.wait("gpsimd", sch.mark("tensor"))
        m_ld = sch.dmas("wl2", lambda g, kt=ktiles, nn=ncols: g.dma_start(
            wbuf[:, 0:kt * nn].rearrange("p (k n) -> p k n", n=nn), _r3(dram)))
        sch.wait("tensor", m_ld)
        return wbuf[:, 0:ktiles * ncols].rearrange("p (k n) -> p k n", n=ncols)

    def projB(dstv, wv, srcv, kk, n_mt, scale=None):
        bmark = [None, None]
        for m0 in range(n_mt):
            for sl in range(4):
                par = sl % 2
                pb = c.ps[par]
                if bmark[par] is not None:
                    sch.wait("tensor", bmark[par])
                for k in range(kk):
                    mmi(c, pb[:, 0:512], wv[:, k, m0 * 128:(m0 + 1) * 128],
                        srcv[:, k, sl * 512:(sl + 1) * 512], k == 0, k == kk - 1,
                        sig=(k == kk - 1))
                sch.wait("scalar", sch.mark("tensor"))
                if scale is None:
                    sch.emit("scalar", lambda e, d=dstv[:, m0, sl * 512:(sl + 1) * 512], p=pb:
                             e.copy(d, p[:, 0:512]), sig=True)
                else:
                    sch.emit("scalar", lambda e, d=dstv[:, m0, sl * 512:(sl + 1) * 512], p=pb,
                             sc=scale: e.activation(d, p[:, 0:512], AF.Copy, scale=sc), sig=True)
                bmark[par] = sch.mark("scalar")
        sch.certify("scalar")
        sch.wait("tensor", sch.mark("scalar"))

    if c.p2_stage < 2:
        pesH.close()
        pesQ.close()
        c.acc = c.sb("acc", [128, NTI * DM], F32)
        sch.barrier()
        pesA.close()
        return
    wq = load_native(c.D["fWq"], DK, DM)
    projB(qv, wq, h1v, DK, DK, scale=0.125)
    if c.p2_stage < 3:
        pesH.close()
        pesQ.close()
        c.acc = c.sb("acc", [128, NTI * DM], F32)
        sch.barrier()
        pesA.close()
        return
    wdkv = load_native(c.D["fWdkv"], DK, LAT)
    projB(cv, wdkv, h1v, DK, 4)
    sch.quiesce(["h1ld"])
    sch.wait("vector", sch.mark("scalar"))
    sch.wait("gpsimd", sch.mark("scalar"))
    pesH.close()
    kT = sb("p2kT", [128, DK * S], BF16, stack=pesQ)
    vaug = sb("p2vaug", [128, NTI * MH * 65], BF16, stack=pesQ)
    kv = kT[:, :].rearrange("p (m t) -> p m t", t=S)
    vv = vaug[:, :].rearrange("p (i h n) -> p i h n", h=MH, n=65)
    if c.p2_stage < 4:
        pesQ.close()
        c.acc = c.sb("acc", [128, NTI * DM], F32)
        sch.barrier()
        pesA.close()
        return
    wuk = load_native(c.D["fWuk"], 4, DM)
    projB(kv, wuk, cv, 4, DK)
    if c.p2_stage < 5:
        pesQ.close()
        c.acc = c.sb("acc", [128, NTI * DM], F32)
        sch.barrier()
        pesA.close()
        return
    wuv = load_native(c.D["fWuv"], 4, DM)
    sch.emit("vector", lambda e: e.memset(vv[:, :, :, 64:65], 1.0), sig=True)
    sch.wait("tensor", sch.mark("vector"))
    vmark = [None, None]
    for i in range(NTI):
        for sl in range(2):
            par = sl % 2
            pb = c.ps[par]
            if vmark[par] is not None:
                sch.wait("tensor", vmark[par])
            for k in range(4):
                mmi(c, pb[:, 0:512], cv[:, k, i * 128:(i + 1) * 128],
                    wuv[:, k, sl * 512:(sl + 1) * 512], k == 0, k == 3, sig=(k == 3))
            sch.wait("vector", sch.mark("tensor"))
            dst = vv[:, i, sl * 8:(sl + 1) * 8, 0:64]
            sch.emit("vector", lambda e, d=dst, p=pb:
                     e.tensor_copy(d, p[:, 0:512].rearrange("p (h n) -> p h n", n=64)),
                     sig=True)
            vmark[par] = sch.mark("vector")
    sch.certify("vector")
    sch.wait("tensor", sch.mark("vector"))

    sq2 = sb("p2sq2", [128, 2048], BF16, stack=pesQ)
    qkmax = sb("p2qkmax", [1, 2 * DK], F32, stack=pesQ)
    tmp1 = sb("p2tmp1", [1, 1], F32, stack=pesQ)
    mubb = sb("p2mubb", [1, DK], BF16, stack=pesQ)
    nmub = sb("p2nmub", [128, DK], F32, stack=pesQ)
    sch.emit("vector", lambda e: e.memset(qkmax[:, :], 0.0), sig=True)
    sch.certify("vector")
    sch.wait("scalar", sch.mark("vector"))
    if not c.p2_mub:
        sch.emits("vector", lambda e: e.memset(nmub[:, :], 0.0))
        sch.certify("vector")
        sch.wait("tensor", sch.mark("vector"))
        sch.wait("scalar", sch.mark("vector"))
    mubmark = None
    for which, src in (((0, qv), (1, kv)) if c.p2_mub else ()):
        for m0 in range(DK):
            sch.wait("scalar", sch.mark("vector"))
            for sl in range(4):
                sch.emit("scalar", lambda e, s=src[:, m0, sl * 512:(sl + 1) * 512], sl=sl:
                         e.activation(sq2[:, sl * 512:(sl + 1) * 512], s, AF.Square),
                         sig=True)
            sch.certify("scalar")
            if mubmark is not None:
                sch.wait("tensor", mubmark)
            sch.wait("tensor", sch.mark("scalar"))
            for sl in range(4):
                PS = c.pA if sl < 2 else c.pB
                mmi(c, PS[0:1, (sl % 2) * 512:(sl % 2) * 512 + 512], c.onesb[:, 0:1],
                    sq2[:, sl * 512:(sl + 1) * 512], True, True, sig=(sl == 3))
            sch.wait("vector", sch.mark("tensor"))
            col = which * DK + m0
            sch.emits("vector", lambda e: e.reduce_max(tmp1[:, :], c.pA[0:1, 0:1024],
                                                       axis=AX.X))
            sch.emits("vector", lambda e, col=col:
                      e.tensor_max(qkmax[:, col:col + 1], qkmax[:, col:col + 1],
                                   tmp1[:, :]))
            sch.emits("vector", lambda e: e.reduce_max(tmp1[:, :], c.pB[0:1, 0:1024],
                                                       axis=AX.X))
            sch.emit("vector", lambda e, col=col:
                     e.tensor_max(qkmax[:, col:col + 1], qkmax[:, col:col + 1],
                                  tmp1[:, :]), sig=True)
            sch.certify("vector")
            mubmark = sch.mark("vector")
    if c.p2_mub:
        sch.emits("vector", lambda e: e.tensor_mul(qkmax[:, 0:DK], qkmax[:, 0:DK],
                                                   qkmax[:, DK:2 * DK]))
        sch.wait("scalar", sch.mark("vector"))
        sch.emit("scalar", lambda e: e.activation(qkmax[:, 0:DK], qkmax[:, 0:DK], AF.Sqrt),
                 sig=True)
        sch.wait("vector", sch.mark("scalar"))
        sch.emit("vector", lambda e: e.tensor_copy(mubb[:, :], qkmax[:, 0:DK]), sig=True)
        sch.certify("vector")
        sch.wait("tensor", sch.mark("vector"))
        mmi(c, c.pw[:, 0:DK], c.onesb[0:1, :], mubb[:, :], True, True, sig=True)
        sch.wait("scalar", sch.mark("tensor"))
        sch.emit("scalar", lambda e: e.activation(nmub[:, :], c.pw[:, 0:DK], AF.Copy,
                                                  scale=-1.0), sig=True)
        sch.certify("scalar")
        sch.wait("tensor", sch.mark("scalar"))
        sch.wait("vector", sch.mark("scalar"))

    pbuf = [sb(f"p2pb{i}", [128, 1024], BF16, stack=pesQ) for i in range(2)]
    rlb = sb("p2rlb", [128, 512], BF16, stack=pesQ)
    rlf = sb("p2rlf", [128, 512], F32, stack=pesQ)
    lb = sb("p2lb", [128, 512], F32, stack=pesQ)
    scmark = [None, None]
    expmark = [None, None]
    pvgate = [None, None]
    pvfree = [None, None]
    bcfree = None
    hIi = [0]
    for h in range(c.p2_heads):
        m0 = h // 2
        pr = 64 * (h % 2)
        for I in range(4):
            qblk = qv[pr:pr + 64, m0, I * 512:(I + 1) * 512]
            npairs = 2 * (I + 1)
            hpar = hIi[0] % 2
            hIi[0] += 1
            oTb = c.pw if hpar == 0 else c.ps[5]

            def emit_sc(p_):
                par = p_ % 2
                PS = c.pA if par == 0 else c.pB
                if scmark[par] is not None:
                    sch.wait("tensor", scmark[par])
                for r in range(2):
                    j = 2 * p_ + r
                    mmi(c, PS[:, r * 512:(r + 1) * 512],
                        kv[pr:pr + 64, m0, j * 128:(j + 1) * 128],
                        qblk, True, True, sig=(r == 1))
                m_sc = sch.mark("tensor")
                if expmark[par] is not None:
                    sch.wait("scalar", expmark[par])
                sch.wait("scalar", m_sc)
                sch.emit("scalar", lambda e, par=par, PS=PS, h=h:
                         e.activation(pbuf[par][:, :], PS[:, 0:1024], AF.Exp,
                                      bias=nmub[:, h // 2:h // 2 + 1]), sig=True)
                sch.certify("scalar")
                scmark[par] = sch.mark("scalar")
                if 2 * p_ - 4 * I >= -1:
                    sch.wait("vector", scmark[par])
                    for r in range(2):
                        d_ = 2 * p_ + r - 4 * I
                        if d_ < 0:
                            continue
                        if d_ > 0:
                            sch.emit("vector", lambda e, r=r, par=par, d_=d_:
                                     e.memset(pbuf[par][:, r * 512:r * 512 + 128 * d_], 0.0))
                        sch.emit("vector", lambda e, r=r, par=par, d_=d_:
                                 e.tensor_tensor(
                                     pbuf[par][:, r * 512 + 128 * d_:r * 512 + 128 * (d_ + 1)],
                                     pbuf[par][:, r * 512 + 128 * d_:r * 512 + 128 * (d_ + 1)],
                                     c.triu[:, :], OP.mult), sig=True)
                    sch.certify("vector")
                    pvgate[par] = sch.mark("vector")
                else:
                    pvgate[par] = scmark[par]

            def emit_pv(p_):
                par = p_ % 2
                sch.wait("tensor", pvgate[par])
                if p_ == 0 and pvfree[hpar] is not None:
                    sch.wait("tensor", pvfree[hpar])
                for r in range(2):
                    j = 2 * p_ + r
                    mmi(c, oTb[0:65, 0:512], vv[:, j, h, 0:65],
                        pbuf[par][:, r * 512:(r + 1) * 512],
                        (p_ == 0 and r == 0), (p_ == npairs - 1 and r == 1),
                        sig=True)
                expmark[par] = sch.mark("tensor")

            emit_sc(0)
            for p_ in range(npairs):
                if p_ + 1 < npairs:
                    emit_sc(p_ + 1)
                emit_pv(p_)
            m_pv = sch.mark("tensor")
            sch.wait("scalar", m_pv)
            sch.emit("scalar", lambda e, oTb=oTb: e.copy(rlb[64:65, 0:512],
                                                         oTb[64:65, 0:512]), sig=True)
            sch.certify("scalar")
            sch.wait("tensor", sch.mark("scalar"))
            if bcfree is not None:
                sch.wait("tensor", bcfree)
            mmi(c, c.ps[4][0:64, 0:512], c.onesb[64:65, 0:64], rlb[64:65, 0:512],
                True, True, sig=True)
            sch.wait("vector", sch.mark("tensor"))
            sch.emits("vector", lambda e: e.reciprocal(lb[0:64, 0:512],
                                                       c.ps[4][0:64, 0:512]))
            sch.certify("vector")
            bcfree = sch.mark("vector")
            dsto = omv[pr:pr + 64, m0, I * 512:(I + 1) * 512]
            sch.emit("vector", lambda e, d=dsto, oTb=oTb:
                     e.tensor_tensor(d, oTb[0:64, 0:512], lb[0:64, 0:512], OP.mult),
                     sig=True)
            sch.certify("vector")
            pvfree[hpar] = sch.mark("vector")

    if c.p2_stage < 7:
        pesQ.close()
        c.acc = c.sb("acc", [128, NTI * DM], F32)
        sch.barrier()
        pesA.close()
        return
    pesQ.close()
    sch.wait("sync", sch.mark("vector"))
    sch.wait("sync", sch.mark("scalar"))
    sch.wait("sync", sch.mark("tensor"))
    c.acc = c.sb("acc", [128, NTI * DM], F32)
    c.h3T = c.sb("h3T", [128, DK * S], BF16)
    wo = load_native(c.D["fWo"], DK, DM)
    xnr = [sb(f"p2xnr{i}", [128, DM], F32) for i in range(2)]
    h3b = sb("p2h3b", [128, DM], BF16)
    sqs = sb("p2sqs", [128, DM], BF16)
    ssq = sb("p2ssq", [128, 1], F32)
    ms = sb("p2ms", [128, 1], F32)
    r3_ = sb("p2r3", [128, 1], F32)
    h3v = c.h3T[:, :].rearrange("p (k t) -> p k t", t=S)
    prev2 = [None, None]
    h2mark = [None, None]
    for i in range(NTI):
        par = i % 2
        if prev2[par] is not None:
            sch.wait("sync", prev2[par])
        m_xn = sch.dmas("xnr", lambda s, i=i, par=par:
                        s.dma_start(xnr[par][:, :],
                                    c.D["xn_sp"].ap()[i * 128:(i + 1) * 128, :]))
        if c.h2mode >= 1:
            for sl in range(2):
                pb = c.ps[sl % 2]
                if h2mark[sl] is not None:
                    sch.wait("tensor", h2mark[sl])
                for k in range(DK):
                    mmi(c, pb[:, 0:512], omv[:, k, i * 128:(i + 1) * 128],
                        wo[:, k, sl * 512:(sl + 1) * 512], k == 0, k == DK - 1,
                        sig=(k == DK - 1))
        if c.h2mode >= 2:
            sch.wait("vector", sch.mark("tensor"))
            sch.wait("vector", m_xn)
            for sl in range(2):
                sch.emit("vector", lambda e, sl=sl, i=i, par=par:
                         e.tensor_add(c.acc[:, i * DM + sl * 512:i * DM + (sl + 1) * 512],
                                      xnr[par][:, sl * 512:(sl + 1) * 512],
                                      c.ps[sl][:, 0:512]), sig=True)
                h2mark[sl] = sch.mark("vector")
        else:
            sch.wait("vector", sch.mark("tensor"))
            sch.emit("vector", lambda e: e.nop(nofuse=True), sig=True)
        m_h2 = sch.mark("vector")
        prev2[par] = m_h2
        if c.p2_stage < 9:
            continue
        sch.wait("scalar", m_h2)
        sch.emit("scalar", lambda e, i=i:
                 e.activation(sqs[:, :], c.acc[:, i * DM:(i + 1) * DM], AF.Square,
                              accum_out=ssq[:, :]), sig=True)
        sch.wait("vector", sch.mark("scalar"))
        sch.emits("vector", lambda e: e.tensor_scalar(ms[:, :], ssq[:, :], 1.0 / DM, 1e-6,
                                                      OP.mult, OP.add))
        sch.emits("vector", lambda e: e.reciprocal(ms[:, :], ms[:, :]))
        sch.wait("scalar", sch.mark("vector"))
        sch.emit("scalar", lambda e: e.activation(r3_[:, :], ms[:, :], AF.Sqrt), sig=True)
        sch.wait("vector", sch.mark("scalar"))
        sch.emit("vector", lambda e, i=i:
                 e.scalar_tensor_tensor(h3b[:, :], c.acc[:, i * DM:(i + 1) * DM], r3_[:, :],
                                        c.g2b[:, :], OP.mult, OP.mult), sig=True)
        sch.certify("vector")
        m_h3 = sch.mark("vector")
        sch.wait("tensor", m_h3)
        sch.wait("tensor", sch.mark("scalar"))
        for k in range(DK):
            tri(c, c.pt[:, 128 * k:128 * (k + 1)], h3b[:, 128 * k:128 * (k + 1)],
                sig=(k == DK - 1))
        sch.wait("scalar", sch.mark("tensor"))
        sch.emit("scalar", lambda e, i=i: e.copy(h3v[:, :, i * 128:(i + 1) * 128], ptv),
                 sig=True)
        sch.certify("scalar")
        sch.wait("vector", sch.mark("scalar"))

    sch.barrier()
    pesA.close()


def dump_sbuf(c, buf, dst):
    sch = c.sch
    st = sch.stream("dump", "sync", 2)
    for i in range(NTI):
        sch.dmas(st, lambda s, i=i:
                 s.dma_start(dst.ap()[i * 128:(i + 1) * 128, :], buf[:, i * DM:(i + 1) * DM]))



def p3_moe(c):
    sch = c.sch
    pes = contextlib.ExitStack()
    sb = lambda n, sh, dt, side="left": c.sb(n, sh, dt, side=side, stack=pes)
    gwb = sb("p3gw", [128, DK * NEXP], BF16, side="left")
    comb = sb("p3comb", [128, NTI * NEXP], F32, side="left")
    lgs = sb("p3lgs", [128, NEXP], F32, side="left")
    top8 = sb("p3top8", [128, NEXP], F32, side="left")
    nm1 = sb("p3nm1", [128, 1], F32, side="left")
    nm2 = sb("p3nm2", [128, 1], F32, side="left")
    keep = sb("p3keep", [128, NEXP], F32, side="left")
    pex = sb("p3pex", [128, NEXP], F32, side="left")
    rs = sb("p3rs", [128, 1], F32, side="left")
    w1b = sb("p3w1", [128, DK * EXPD], BF16, side="left")
    w2b = sb("p3w2", [128, 16 * DM], BF16, side="left")
    hid = sb("p3hid", [128, 16 * 512], BF16, side="left")

    h3v = c.h3T[:, :].rearrange("p (k t) -> p k t", t=S)
    gv = gwb[:, :].rearrange("p (k n) -> p k n", n=NEXP)
    w1v = w1b[:, :].rearrange("p (k n) -> p k n", n=EXPD)
    w2v = w2b[:, :].rearrange("p (k n) -> p k n", n=DM)
    hv = hid[:, :].rearrange("p (m n) -> p m n", n=512)
    cbv = comb[:, :].rearrange("p (i n) -> p i n", n=NEXP)

    sch.stream("wl3", "gpsimd", 2)
    m_gw = sch.dmas("wl3", lambda g: g.dma_start(gv, _r3(c.D["gate_w"])))
    sch.wait("tensor", m_gw)

    for i in range(NTI):
        sch.wait("tensor", sch.mark("vector"))
        sch.wait("tensor", sch.mark("scalar"))
        for k in range(DK):
            mmi(c, c.pw[:, 0:NEXP], h3v[:, k, i * 128:(i + 1) * 128], gv[:, k, :],
                k == 0, k == DK - 1, sig=(k == DK - 1))
        sch.wait("vector", sch.mark("tensor"))
        v = "vector"
        sch.emits(v, lambda e: e.tensor_copy(lgs[:, :], c.pw[:, 0:NEXP]))
        sch.emits(v, lambda e: e.max(out=top8[:, :], in_=lgs[:, :]))
        sch.emits(v, lambda e: e.tensor_scalar_mul(nm1[:, :], top8[:, 0:1], -1.0))
        sch.emits(v, lambda e: e.tensor_scalar_mul(nm2[:, :], top8[:, 1:2], -1.0))
        sch.emits(v, lambda e: e.tensor_scalar(keep[:, :], lgs[:, :], nm2[:, :], 0.0,
                                               OP.add, OP.is_ge))
        sch.wait("scalar", sch.mark(v))
        sch.emit("scalar", lambda e: e.activation(pex[:, :], lgs[:, :], AF.Exp,
                                                  bias=nm1[:, :]), sig=True)
        sch.wait(v, sch.mark("scalar"))
        sch.emits(v, lambda e: e.tensor_mul(pex[:, :], pex[:, :], keep[:, :]))
        sch.emits(v, lambda e: e.reduce_sum(rs[:, :], pex[:, :], axis=AX.X))
        sch.emits(v, lambda e: e.reciprocal(rs[:, :], rs[:, :]))
        sch.emit(v, lambda e, i=i: e.tensor_scalar_mul(cbv[:, i, :], pex[:, :], rs[:, :]),
                 sig=True)
        sch.certify(v)

    m_w1 = m_w2 = None
    hmark = [None, None]
    ymark = [None, None]
    for e_ in range(NEXP):
        if e_ == 0:
            m_w1 = sch.dmas("wl3", lambda g, e_=e_: g.dma_start(w1v, _r3(c.D[f"w1_{e_}"])))
            m_w2 = sch.dmas("wl3", lambda g, e_=e_: g.dma_start(w2v, _r3(c.D[f"w2_{e_}"])))
        sch.wait("tensor", m_w1)
        m_hid_last = None
        for sl in range(4):
            for m in range(16):
                par = m % 2
                pb = c.ps[par]
                if hmark[par] is not None:
                    sch.wait("tensor", hmark[par])
                for k in range(DK):
                    mmi(c, pb[:, 0:512], w1v[:, k, m * 128:(m + 1) * 128],
                        h3v[:, k, sl * 512:(sl + 1) * 512], k == 0, k == DK - 1,
                        sig=(k == DK - 1))
                sch.wait("scalar", sch.mark("tensor"))
                sch.emit("scalar", lambda e, m=m, pb=pb:
                         e.activation(hv[:, m, :], pb[:, 0:512], AF.Silu), sig=True)
                hmark[par] = sch.mark("scalar")
            sch.certify("scalar")
            m_hid_last = sch.mark("scalar")
            if sl == 3 and e_ < NEXP - 1:
                sch.wait("gpsimd", sch.mark("tensor"))
                m_w1 = sch.dmas("wl3", lambda g, e_=e_: g.dma_start(w1v, _r3(c.D[f"w1_{e_ + 1}"])))
            sch.wait("tensor", m_w2)
            sch.wait("tensor", m_hid_last)
            for tt in range(4):
                i = sl * 4 + tt
                for dsl in range(2):
                    ypar = (tt * 2 + dsl) % 2
                    pb = c.ps[2 + ypar]
                    if ymark[ypar] is not None:
                        sch.wait("tensor", ymark[ypar])
                    for m in range(16):
                        mmi(c, pb[:, 0:512], hv[:, m, tt * 128:(tt + 1) * 128],
                            w2v[:, m, dsl * 512:(dsl + 1) * 512], m == 0, m == 15,
                            sig=(m == 15))
                    sch.wait("vector", sch.mark("tensor"))
                    sch.emit("vector", lambda e, i=i, dsl=dsl, pb=pb, e_=e_:
                             e.scalar_tensor_tensor(
                                 c.acc[:, i * DM + dsl * 512:i * DM + (dsl + 1) * 512],
                                 pb[:, 0:512], comb[:, i * NEXP + e_:i * NEXP + e_ + 1],
                                 c.acc[:, i * DM + dsl * 512:i * DM + (dsl + 1) * 512],
                                 OP.mult, OP.add), sig=True)
                    ymark[ypar] = sch.mark("vector")
        if e_ < NEXP - 1:
            sch.wait("gpsimd", sch.mark("tensor"))
            m_w2 = sch.dmas("wl3", lambda g, e_=e_: g.dma_start(w2v, _r3(c.D[f"w2_{e_ + 1}"])))
    sch.barrier()
    pes.close()



def p4_final(c):
    sch = c.sch
    sch.stream("x4", "sync", 2)
    sch.stream("s4", "sync", 2)
    sch.stream("y", "sync", 2)
    pes = contextlib.ExitStack()
    sb = lambda n, sh, dt, side="left": c.sb(n, sh, dt, side=side, stack=pes)
    gf = [sb(f"p4g{i}", [128, DM], F32, side="left") for i in range(2)]
    xr = [sb(f"p4x{i}", [128, DM], F32, side="left") for i in range(2)]
    sr = [sb(f"p4s{i}", [128, DM], F32, side="left") for i in range(2)]
    yt = [sb(f"p4y{i}", [128, DM], F32, side="left") for i in range(2)]
    prev = [None, None]
    prev_y = [None, None]
    for i in range(NTI):
        par = i % 2
        if prev[par] is not None:
            sch.wait("sync", prev[par])
        m_inx = sch.dmas("x4", lambda s, i=i, par=par:
                         s.dma_start(xr[par][:, :], c.D["x"].ap()[i * 128:(i + 1) * 128, :]))
        m_ins = sch.dmas("s4", lambda s, i=i, par=par:
                         s.dma_start(sr[par][:, :],
                                     c.D["s_sp"].ap()[i * 128:(i + 1) * 128, :]))
        sch.wait("scalar", sch.mark("vector"))
        sch.emit("scalar", lambda e, i=i, par=par:
                 e.activation(gf[par][:, :], c.acc[:, i * DM:(i + 1) * DM], AF.Silu),
                 sig=True)
        sch.certify("scalar")
        sch.wait("vector", sch.mark("scalar"))
        sch.wait("vector", m_inx)
        sch.wait("vector", m_ins)
        if prev_y[par] is not None:
            sch.wait("vector", prev_y[par])
        sch.emit("vector", lambda e, par=par: e.tensor_mul(gf[par][:, :], gf[par][:, :],
                                                           sr[par][:, :]))
        sch.emit("vector", lambda e, par=par: e.tensor_add(yt[par][:, :], xr[par][:, :],
                                                           gf[par][:, :]), sig=True)
        m_y = sch.mark("vector")
        prev[par] = m_y
        sch.wait("sync", m_y)
        prev_y[par] = sch.dmas("y", lambda s, i=i, par=par:
                               s.dma_start(c.D["y"].ap()[i * 128:(i + 1) * 128, :],
                                           yt[par][:, :]))
    if "g" in c.taps:
        pass
    pes.close()


def finish(c):
    sch = c.sch
    if "xn" in c.taps:
        tap_copy_dram(c, c.D["xn_sp"], c.taps["xn"], "txn")
    if "h1" in c.taps:
        tap_copy_dram(c, c.D["h1_sp"], c.taps["h1"], "th1")
    if "s" in c.taps:
        tap_copy_dram(c, c.D["s_sp"], c.taps["s"], "ts")
    if "h2" in c.taps:
        dump_sbuf(c, c.acc, c.taps["h2"])
    if "h4" in c.taps:
        dump_sbuf(c, c.acc, c.taps["h4"])
    for name, (eng, sems, cnts, idx) in c.sch.streams.items():
        for m in sch.stream_done(name):
            sch.wait(eng, m)
    sch.barrier()



_CACHE = {}


def _get_nc(taps=(), upto=4):
    key = (tuple(taps), upto)
    if key not in _CACHE:
        _CACHE[key] = build_program(taps, upto)
    return _CACHE[key]


def make_in_maps(inputs):
    x = np.ascontiguousarray(inputs["L0_attn_hidden_states"], dtype=np.float32)
    norm_w = inputs["L0_attn_norm_w"]
    base = {
        "aWq": inputs["L0_attn_Wq"], "aWk": inputs["L0_attn_Wk"],
        "aWv": inputs["L0_attn_Wv"], "aWo": inputs["L0_attn_Wo"],
        "normw_tok": np.ascontiguousarray(
            np.broadcast_to(np.tile(np.asarray(norm_w, np.float32), AH), (128, DM))),
        "g1b": np.ascontiguousarray(np.broadcast_to(np.asarray(inputs["L0_ffn_g1"], np.float32), (128, DM))),
        "g2b": np.ascontiguousarray(np.broadcast_to(np.asarray(inputs["L0_ffn_g2"], np.float32), (128, DM))),
        "fWq": inputs["L0_ffn_Wq"], "fWdkv": inputs["L0_ffn_Wdkv"],
        "fWuk": inputs["L0_ffn_Wuk"], "fWuv": inputs["L0_ffn_Wuv"],
        "fWo": inputs["L0_ffn_Wo"], "gate_w": inputs["L0_ffn_gate_w"],
        "c_id": np.eye(128, dtype=np.float32),
        "c_triu": np.triu(np.ones((128, 128), dtype=np.float32)),
        "c_ones": np.ones((128, 128), dtype=np.float32),
    }
    for e in range(NEXP):
        base[f"w1_{e}"] = np.ascontiguousarray(inputs["L0_ffn_w1"][e])
        base[f"w2_{e}"] = np.ascontiguousarray(inputs["L0_ffn_w2"][e])
    base = {k: np.ascontiguousarray(np.asarray(v, dtype=np.float32)) for k, v in base.items()}
    maps = []
    for b in range(8):
        m = dict(base)
        m["x"] = np.ascontiguousarray(x[b])
        maps.append(m)
    return maps


def kernel(**inputs):
    nc, c = _get_nc()
    maps = make_in_maps(inputs)
    res = run_bass_kernel_spmd(nc, maps, core_ids=list(range(8)))
    return np.stack([r["y"] for r in res.results], axis=0)
